# version 49 (final)
# baseline (speedup 1.0000x reference)
import numpy as np
import ml_dtypes
from contextlib import ExitStack

import concourse.bass as bass
import concourse.mybir as mybir
from concourse.bass_utils import run_bass_kernel_spmd

F32 = mybir.dt.float32
BF16 = mybir.dt.bfloat16
I32 = mybir.dt.int32
AF = mybir.ActivationFunctionType
ALU = mybir.AluOpType
AX = mybir.AxisListType
bf16 = ml_dtypes.bfloat16

D = 1024
DI = 2048
DS = 16
KD = D // 128
KI = DI // 128
NE = 8
EPS = 1e-5
DECAY = 0.9
ENGS = ("sync", "tensor", "vector", "scalar", "gpsimd")


class Sem:
    __slots__ = ("h", "n", "name")

    def __init__(self, h, name):
        self.h = h
        self.n = 0
        self.name = name

    def bump(self, k):
        self.n += k
        return (self, self.n)

    def up(self, k=1):
        self.n += k
        return (self, k)

    def tok(self):
        return (self, self.n)


class Plan:
    def __init__(self):
        self.ops = {e: [] for e in ENGS}

    def add(self, eng, fn, waits=(), inc=None, drain=False):
        ws = []
        for w in waits:
            if w is None:
                continue
            if isinstance(w, list):
                ws.extend(x for x in w if x is not None)
            else:
                ws.append(w)
        self.ops[eng].append((fn, ws, inc, drain))

    def replay(self, eng_name, e):
        for fn, waits, inc, drain in self.ops[eng_name]:
            for s, v in waits:
                e.wait_ge(s.h, v)
            ins = fn(e)
            if inc is not None:
                ins.then_inc(inc[0].h, inc[1])
            if drain:
                e.drain()


def build_program(S=4096, T1=256, T2=512):
    NC1 = S // T1
    NC2 = S // T2
    NT1 = T1 // 128
    NT2 = T2 // 128
    M1G = 4 * KD

    nc = bass.Bass("TRN2", target_bir_lowering=False, debug=False)

    d_x = nc.dram_tensor("x", [S, D], F32, kind="ExternalInput").ap()
    d_win = nc.dram_tensor("w_in", [D, 2 * DI], F32, kind="ExternalInput").ap()
    d_wout = nc.dram_tensor("w_out", [DI, D], F32, kind="ExternalInput").ap()
    d_wB = nc.dram_tensor("w_B", [DI, DS], F32, kind="ExternalInput").ap()
    d_wC = nc.dram_tensor("w_C", [DS, DI], F32, kind="ExternalInput").ap()
    d_convw = nc.dram_tensor("conv_fm", [128, KI * 4], F32, kind="ExternalInput").ap()
    d_convb = nc.dram_tensor("convb_fm", [128, KI], F32, kind="ExternalInput").ap()
    d_Wr = nc.dram_tensor("Wr", [D, NE], F32, kind="ExternalInput").ap()
    d_br = nc.dram_tensor("br", [1, NE], F32, kind="ExternalInput").ap()
    d_W1 = nc.dram_tensor("W1", [NE, D, D], F32, kind="ExternalInput").ap()
    d_W2 = nc.dram_tensor("W2", [NE, D, D], F32, kind="ExternalInput").ap()
    d_b1rl = nc.dram_tensor("b1rl", [128, KD * NE], F32, kind="ExternalInput").ap()
    d_b2rl = nc.dram_tensor("b2rl", [128, KD * NE], F32, kind="ExternalInput").ap()
    d_id32 = nc.dram_tensor("id32", [DS, DS], F32, kind="ExternalInput").ap()
    d_id16 = nc.dram_tensor("id16", [128, 128], BF16, kind="ExternalInput").ap()
    d_xt = nc.dram_tensor("xt_scr", [S, D], F32, kind="Internal").ap()
    d_y = nc.dram_tensor("y", [S, D], F32, kind="ExternalOutput").ap()

    P = Plan()
    es = ExitStack()
    sem_names = [
        "sga", "sgb", "sgc", "sgd", "sge", "sgf", "gld", "gdg", "gid", "wcz", "gio", "wcvt", "x0", "xa", "xb", "xcq", "spt", "spv", "sps", "route",
        "norm", "tT", "tTe", "m1", "m1ev", "m1es",
        "conv", "xc", "m2", "duv", "Hv", "m3", "y2", "m4", "o1e",
        "to1", "xte", "xtst", "xtsA", "p1dt", "p1dv", "p1ds",
        "gw1a", "gw1b", "gw2a", "gw2b", "x2a0", "x2a1", "x2a2", "x2a3", "x2b0", "x2b1", "x2b2", "x2b3", "x2c0", "x2c1", "x2c2", "x2c3", "norm2", "tT2", "tT2e",
        "m5", "gel", "m6", "oev", "to2", "oute", "yst",
    ]
    SM = {n: Sem(es.enter_context(nc.semaphore(name="sem_" + n)), n) for n in sem_names}

    def T(fn, waits=(), inc=None):
        P.add("tensor", fn, waits, inc, drain=False)

    def V(fn, waits=(), inc=None, drain=True):
        P.add("vector", fn, waits, inc, drain=drain)

    def C(fn, waits=(), inc=None):
        P.add("scalar", fn, waits, inc, drain=True)

    def G(fn, waits=(), inc=None, drain=True):
        P.add("gpsimd", fn, waits, inc, drain=drain)

    def DMA(eng, fn, waits=(), sem=None):
        tok = sem.bump(16)
        P.add(eng, fn, waits, (sem, 16), drain=False)
        return tok

    sb = lambda name, shape, dt: es.enter_context(nc.sbuf_tensor(name, shape, dt))
    b1rl = sb("b1rl_s", [128, KD * NE], F32)
    b2rl = sb("b2rl_s", [128, KD * NE], F32)
    id32 = sb("id32_s", [DS, DS], F32)
    id16 = sb("id16_s", [128, 128], BF16)
    br_s = sb("br_s", [1, NE], F32)
    iot8 = sb("iot8", [1, NE], F32)
    iotp = sb("iotp", [128, 1], I32)
    ones_r = sb("ones_r", [1, 128], F32)
    ones_c = sb("ones_c", [128, 1], F32)
    gidx = sb("gidx", [128, 2 * KD], I32)
    oh_bc = sb("oh_bc", [128, 2 * NE], F32)
    b1g = sb("b1g", [128, 2 * KD], F32)
    b2sum = sb("b2sum", [128, KD], F32)
    b2g1 = sb("b2g1", [128, KD], F32)
    st12 = sb("st12", [128, 12], F32)
    mAll = sb("mAll", [128, 4], F32)
    vAll = sb("vAll", [128, 4], F32)
    rsA = sb("rsA", [128, 4], F32)
    nwt = sb("nwt", [128, 4], F32)
    tmp8g = sb("tmp8g", [128, NE], F32)
    mv = sb("mv", [128, 2], F32)
    csc = sb("csc", [1, 1], F32)

    wsc = ExitStack()
    wsb = lambda name, shape, dt: wsc.enter_context(nc.sbuf_tensor(name, shape, dt))
    w_in_bf = wsb("w_in_bf", [128, KD * 2 * DI], BF16)
    w_out_bf = wsb("w_out_bf", [128, KI * D], BF16)
    w_B_bf = wsb("w_B_bf", [128, KI * DS], BF16)
    w_C_bf = wsb("w_C_bf", [128, DI], BF16)
    conv_w = wsb("conv_w", [128, KI * 4], F32)
    conv_b = wsb("conv_b", [128, KI], F32)
    hcarry = wsb("hcarry", [DS, 1], F32)
    c09 = wsb("c09", [DS, T1], F32)
    diag_bf = wsb("diag_bf", [128, 4 * KI * 128], BF16)
    x_tm = wsb("x_tm", [128, 3 * NT1 * D], F32)
    ln1_tm = wsb("ln1_tm", [128, 2 * NT1 * D], BF16)
    ln1_fm = wsb("ln1_fm", [128, 2 * KD * T1], BF16)

    ps = lambda name, shape, dt: es.enter_context(nc.psum_tensor(name, shape, dt))
    pA = [ps("pA0", [128, 512], F32), ps("pA1", [128, 512], F32)]
    pT16 = [ps("pT16a", [128, 1024], BF16), ps("pT16b", [128, 1024], BF16)]
    pT32 = [ps("pT32a", [128, 512], F32), ps("pT32b", [128, 512], F32)]
    pUY = [ps("pUY0", [128, 512], F32), ps("pUY1", [128, 512], F32)]
    PB = [pA[0], pA[1], pT32[0], pT32[1]]
    bank_guard = {}

    def bwait(bank_id):
        return bank_guard.get(bank_id)

    def bset(bank_id, tok):
        bank_guard[bank_id] = tok

    t_id32 = DMA("gpsimd", lambda e: e.dma_start(out=id32[:], in_=d_id32), sem=SM["gid"])
    t_id16 = DMA("gpsimd", lambda e: e.dma_start(out=id16[:], in_=d_id16), sem=SM["gid"])
    t_gid = SM["gid"].tok()
    t_convw = DMA("gpsimd", lambda e: e.dma_start(out=conv_w[:], in_=d_convw), sem=SM["gld"])
    t_convb = DMA("gpsimd", lambda e: e.dma_start(out=conv_b[:], in_=d_convb), sem=SM["gld"])
    t_b1rl = DMA("gpsimd", lambda e: e.dma_start(out=b1rl[:], in_=d_b1rl), sem=SM["gld"])
    t_b2rl = DMA("gpsimd", lambda e: e.dma_start(out=b2rl[:], in_=d_b2rl), sem=SM["gld"])
    t_br = DMA("gpsimd", lambda e: e.dma_start(out=br_s[:], in_=d_br), sem=SM["gld"])
    t_wBbf = None
    for k in range(KI):
        t_wBbf = DMA("gpsimd", lambda e, k=k: e.dma_start(
            out=w_B_bf[:, k * DS:(k + 1) * DS], in_=d_wB[k * 128:(k + 1) * 128, :]),
            sem=SM["gld"])
    V(lambda e: e.memset(w_C_bf[:], 0.0), inc=SM["wcz"].up())
    t_wCbf = DMA("gpsimd", lambda e: e.dma_start(out=w_C_bf[0:DS, :], in_=d_wC),
                 waits=[SM["wcz"].tok()], sem=SM["gld"])

    G(lambda e: e.iota(iot8[:], pattern=[[1, NE]], base=0, channel_multiplier=0,
                       allow_small_or_imprecise_dtypes=True), inc=SM["gio"].up())
    G(lambda e: e.iota(iotp[:], pattern=[[0, 1]], base=0, channel_multiplier=1),
      inc=SM["gio"].up())
    t_iota = SM["gio"].tok()
    V(lambda e: e.memset(ones_r[:], 1.0))
    V(lambda e: e.memset(ones_c[:], 1.0))
    V(lambda e: e.memset(c09[:], DECAY))
    V(lambda e: e.memset(hcarry[:], 0.0), inc=SM["spv"].up())
    t_memset = SM["spv"].tok()

    XPW = T1 + 3
    x_tok = {}
    x_tile_tok = {}
    guard = {}
    norm_tok = {}
    tTe_tok = {}
    m1_tok = {}
    edge_tok = {}
    m2_tok = {}
    m4_tok = {}
    y2_tok = {}
    xc_tok = {}
    xte_tok = {}

    def load_x(c):
        if c >= NC1:
            return
        par = (c + 2) % 3
        toks = []
        for tt in range(NT1):
            toks.append(DMA("sync", lambda e, c=c, tt=tt, par=par: e.dma_start(
                out=x_tm[:, (par * NT1 + tt) * D:(par * NT1 + tt + 1) * D],
                in_=d_x[c * T1 + tt * 128: c * T1 + (tt + 1) * 128, :]),
                waits=[guard.get(("x_tm", par))], sem=SM[("xa", "xb", "xcq")[par]]))
        x_tok[c] = toks[-1]
        x_tile_tok[c] = toks

    def plan_norm(c):
        if c >= NC1:
            return
        par = (c + 2) % 3
        base = par * NT1 * D
        lt0 = (c % 2) * NT1 * D
        for tt in range(NT1):
            V(lambda e, tt=tt, base=base: e.bn_stats(
                st12[:, 0:6], x_tm[:, base + tt * D: base + tt * D + 512]),
              waits=[x_tok[c] if tt == 0 else None,
                     guard.get(("ln1_tm", c % 2)) if tt == 0 else None])
            V(lambda e, tt=tt, base=base: e.bn_stats(
                st12[:, 6:12], x_tm[:, base + tt * D + 512: base + (tt + 1) * D]))
            V(lambda e: e.bn_aggr(mv[:], st12[:]))
            V(lambda e, tt=tt: e.tensor_copy(mAll[:, tt:tt + 1], mv[:, 0:1]))
            V(lambda e, tt=tt: e.tensor_scalar_add(vAll[:, tt:tt + 1], mv[:, 1:2], EPS))
        NT = NT1
        V(lambda e, NT=NT: e.tensor_scalar(rsA[:, 0:NT].bitcast(I32),
                                           vAll[:, 0:NT].bitcast(I32), 1,
                                           None, op0=ALU.arith_shift_right))
        V(lambda e, NT=NT: e.tensor_scalar(rsA[:, 0:NT].bitcast(I32),
                                           rsA[:, 0:NT].bitcast(I32), -1,
                                           0x5f3759df, op0=ALU.mult, op1=ALU.add))
        for _ in range(3):
            V(lambda e, NT=NT: e.tensor_tensor(nwt[:, 0:NT], rsA[:, 0:NT], rsA[:, 0:NT], op=ALU.mult))
            V(lambda e, NT=NT: e.tensor_tensor(nwt[:, 0:NT], nwt[:, 0:NT], vAll[:, 0:NT], op=ALU.mult))
            V(lambda e, NT=NT: e.tensor_scalar(nwt[:, 0:NT], nwt[:, 0:NT], -0.5, 1.5,
                                               op0=ALU.mult, op1=ALU.add))
            V(lambda e, NT=NT: e.tensor_tensor(rsA[:, 0:NT], rsA[:, 0:NT], nwt[:, 0:NT], op=ALU.mult))
        for tt in range(NT1):
            V(lambda e, tt=tt, base=base, lt0=lt0: e.tensor_scalar(
                ln1_tm[:, lt0 + tt * D: lt0 + (tt + 1) * D],
                x_tm[:, base + tt * D: base + (tt + 1) * D],
                mAll[:, tt:tt + 1], rsA[:, tt:tt + 1],
                op0=ALU.subtract, op1=ALU.mult),
              inc=SM["norm"].up(), drain=False)
        norm_tok[c] = SM["norm"].tok()

    def plan_T(c):
        if c >= NC1:
            return
        par = c % 2
        fm0 = par * KD * T1
        lt0 = par * NT1 * D
        for tt in range(NT1):
            b = tt % 2
            for k in range(KD):
                T(lambda e, tt=tt, k=k, b=b, lt0=lt0: e.transpose(
                    pT16[b][:, k * 128:(k + 1) * 128],
                    ln1_tm[:, lt0 + tt * D + k * 128: lt0 + tt * D + (k + 1) * 128], id16[:]),
                  waits=[norm_tok[c], t_gid, bwait(("T16", b))] if k == 0 else (),
                  inc=SM["tT"].up() if k == KD - 1 else None)
            tk = SM["tT"].tok()
            dst = ln1_fm[:, fm0:fm0 + KD * T1].rearrange(
                "p (k t) -> p k t", t=T1)[:, :, tt * 128:(tt + 1) * 128]
            srcv = pT16[b][:, 0:KD * 128].rearrange("p (k t) -> p k t", t=128)
            V(lambda e, dst=dst, srcv=srcv: e.tensor_copy(dst, srcv),
              waits=[tk, guard.get(("ln1_fm", par)) if tt == 0 else None],
              inc=SM["tTe"].up(), drain=False)
            bset(("T16", b), SM["tTe"].tok())
        tTe_tok[c] = SM["tTe"].tok()
        guard[("ln1_tm", par)] = SM["tT"].tok()


    with ExitStack() as side:
        ssb = lambda name, shape, dt: side.enter_context(nc.sbuf_tensor(name, shape, dt))
        stg = [ssb("stg%d" % i, [128, DI], F32) for i in range(4)]
        SLOTQ = ("sync", "sync", "gpsimd", "gpsimd")
        SLOTS = ("sga", "sgb", "sgc", "sgd")
        row_buf = ssb("row_buf", [1, 2048], F32)
        wB32 = ssb("wB32", [128, KI * DS], F32)
        Wr32 = ssb("Wr32", [128, KD * NE], F32)
        row4 = ssb("row4", [1, 8], F32)
        rowm = ssb("rowm", [1, 2], F32)
        bc2 = ssb("bc2", [128, 2], F32)
        x0f = ssb("x0f", [128, KD], F32)
        ln0 = ssb("ln0", [128, KD], F32)
        xc0 = ssb("xc0", [128, KI], F32)
        tm0 = ssb("tm0", [128, KI], F32)
        h1 = ssb("h1", [DS, 1], F32)
        y20 = ssb("y20", [128, KI], F32)
        xt0 = ssb("xt0", [128, KD], F32)
        xt0sq = ssb("xt0sq", [128, KD], F32)
        ln20 = ssb("ln20", [128, KD], F32)
        sc_row = ssb("sc_row", [1, NE], F32)
        mx8 = ssb("mx8", [1, NE], F32)
        msk8 = ssb("msk8", [1, NE], F32)
        idxf = ssb("idxf", [1, 2], F32)
        ebc = ssb("ebc", [128, 2], I32)
        ebase = ssb("ebase", [128, 2], I32)
        tmp8 = ssb("tmp8", [128, NE], F32)
        x0r = row_buf[0:1, 0:D]

        for k in range(KI):
            t_wB32 = DMA("gpsimd", lambda e, k=k: e.dma_start(
                out=wB32[:, k * DS:(k + 1) * DS], in_=d_wB[k * 128:(k + 1) * 128, :]),
                sem=SM["gld"])
        t_Wr = None
        for k in range(KD):
            t_Wr = DMA("gpsimd", lambda e, k=k: e.dma_start(
                out=Wr32[:, k * NE:(k + 1) * NE], in_=d_Wr[k * 128:(k + 1) * 128, :]),
                sem=SM["gld"])
        t_x0 = DMA("sync", lambda e: e.dma_start(out=x0r[:], in_=d_x[0:1, :]), sem=SM["x0"])
        t_gld = SM["gld"].tok()
        load_x(0)
        load_x(1)

        V(lambda e: e.bn_stats(st12[0:1, 0:6], x0r[0:1, 0:512]), waits=[t_x0])
        V(lambda e: e.bn_stats(st12[0:1, 6:12], x0r[0:1, 512:1024]))
        V(lambda e: e.bn_aggr(mv[0:1, :], st12[0:1, :]))
        V(lambda e: e.tensor_scalar_add(row4[0:1, 0:1], mv[0:1, 1:2], EPS))
        V(lambda e: e.tensor_scalar(row4[0:1, 1:2].bitcast(I32),
                                    row4[0:1, 0:1].bitcast(I32), 1, None,
                                    op0=ALU.arith_shift_right))
        V(lambda e: e.tensor_scalar(row4[0:1, 1:2].bitcast(I32),
                                    row4[0:1, 1:2].bitcast(I32), -1,
                                    0x5f3759df, op0=ALU.mult, op1=ALU.add))
        for _ in range(3):
            V(lambda e: e.tensor_tensor(row4[0:1, 2:3], row4[0:1, 1:2], row4[0:1, 1:2], op=ALU.mult))
            V(lambda e: e.tensor_tensor(row4[0:1, 2:3], row4[0:1, 2:3], row4[0:1, 0:1], op=ALU.mult))
            V(lambda e: e.tensor_scalar(row4[0:1, 2:3], row4[0:1, 2:3], -0.5, 1.5,
                                        op0=ALU.mult, op1=ALU.add))
            V(lambda e: e.tensor_tensor(row4[0:1, 1:2], row4[0:1, 1:2], row4[0:1, 2:3], op=ALU.mult))
        V(lambda e: e.tensor_scalar(row_buf[0:1, D:2 * D], x0r[:], mv[0:1, 0:1],
                                    row4[0:1, 1:2], op0=ALU.subtract, op1=ALU.mult),
          inc=SM["spv"].up())
        t_lnrow = SM["spv"].tok()
        for k in range(KD):
            T(lambda e, k=k: e.transpose(pUY[0][:, k:k + 1],
                                         row_buf[0:1, D + k * 128: D + (k + 1) * 128],
                                         ones_r[0:1, 0:1]),
              waits=[t_lnrow, t_gid, t_memset] if k == 0 else (), inc=SM["spt"].up())
        t_lnt = SM["spt"].tok()
        V(lambda e: e.tensor_copy(ln0[:], pUY[0][:, 0:KD]), waits=[t_lnt],
          inc=SM["spv"].up())
        t_ln0 = SM["spv"].tok()
        bset(("UY", 0), t_ln0)
        for k in range(KD):
            T(lambda e, k=k: e.transpose(pUY[1][:, 8 + k:8 + k + 1], x0r[0:1, k * 128:(k + 1) * 128],
                                         ones_r[0:1, 0:1]),
              waits=[t_x0, t_gid, t_memset] if k == 0 else (), inc=SM["spt"].up())
        t_x0t = SM["spt"].tok()
        V(lambda e: e.tensor_copy(x0f[:], pUY[1][:, 8:8 + KD]), waits=[t_x0t],
          inc=SM["spv"].up())
        t_x0f = SM["spv"].tok()
        bset(("UY", 1), t_x0f)

        plan_norm(0)
        plan_norm(1)
        for t in range(4 * KI):
            V(lambda e, t=t: e.tensor_scalar(
                diag_bf[:, t * 128:(t + 1) * 128], id16[:], conv_w[:, t:t + 1],
                None, op0=ALU.mult),
              waits=[t_gld, t_gid] if t == 0 else (),
              inc=SM["gdg"].up() if t == 4 * KI - 1 else None)
        t_gdg = SM["gdg"].tok()

        RB = [pA[0], pA[1], pT32[0], pT32[1]]
        for hf in range(2):
            for k in range(KD):
                idx = hf * KD + k
                sidx = idx % 4
                t_st = DMA(SLOTQ[sidx], lambda e, k=k, hf=hf, sidx=sidx: e.dma_start(
                    out=stg[sidx][:], in_=d_win[k * 128:(k + 1) * 128, hf * DI:(hf + 1) * DI]),
                    waits=[bwait(("stg", sidx))], sem=SM[SLOTS[sidx]])
                C(lambda e, k=k, hf=hf, sidx=sidx: e.copy(
                    w_in_bf[:, k * 2 * DI + hf * DI: k * 2 * DI + (hf + 1) * DI], stg[sidx][:]),
                  waits=[t_st], inc=SM["wcvt"].up())
                t_cvt = SM["wcvt"].tok()
                for q in range(4):
                    T(lambda e, k=k, q=q, sidx=sidx: e.matmul(
                        RB[q][0:1, 0:512], ln0[:, k:k + 1],
                        stg[sidx][:, q * 512:(q + 1) * 512],
                        start=(k == 0), stop=(k == KD - 1)),
                      waits=[t_st if q == 0 else None,
                             t_ln0 if (idx == 0 and q == 0) else None,
                             bwait(("A", q)) if k == 0 else None],
                      inc=SM["spt"].up() if q == 3 else None)
                bset(("stg", sidx), [SM["spt"].tok(), t_cvt])
            tmm = SM["spt"].tok()
            for q in range(4):
                V(lambda e, q=q: e.tensor_copy(row_buf[0:1, q * 512:(q + 1) * 512],
                                               RB[q][0:1, 0:512]),
                  waits=[tmm, bwait(("rowb", 0))] if q == 0 else (),
                  inc=SM["spv"].up() if q == 3 else None)
            t_rowe = SM["spv"].tok()
            for q in range(4):
                bset(("A", q), t_rowe)
            for m in range(KI):
                T(lambda e, m=m, hf=hf: e.transpose(
                    pUY[1][:, 64 + hf * KI + m: 64 + hf * KI + m + 1],
                    row_buf[0:1, m * 128:(m + 1) * 128], ones_r[0:1, 0:1]),
                  waits=([t_rowe] + ([bwait(("UY", 1))] if hf == 0 else [])) if m == 0 else (),
                  inc=SM["spt"].up() if m == KI - 1 else None)
            bset(("rowb", 0), SM["spt"].tok())
        t_xz0 = SM["spt"].tok()
        t_wincvt = SM["wcvt"].tok()

        V(lambda e: e.tensor_tensor(tm0[:], pUY[1][:, 64:64 + KI],
                                    conv_w[:, 3::4], op=ALU.mult),
          waits=[t_xz0, t_gld])
        V(lambda e: e.tensor_tensor(tm0[:], tm0[:], conv_b[:, 0:KI], op=ALU.add),
          waits=[t_gld], inc=SM["spv"].up())
        t_c0 = SM["spv"].tok()
        C(lambda e: e.activation(xc0[:], tm0[:], AF.Silu),
          waits=[t_c0], inc=SM["sps"].up())
        t_xc0 = SM["sps"].tok()
        C(lambda e: e.activation(tm0[:], pUY[1][:, 64 + KI:64 + 2 * KI], AF.Silu),
          waits=[t_xz0], inc=SM["sps"].up())
        t_sz0 = SM["sps"].tok()
        bset(("UY", 1), [t_c0, t_sz0])
        for k in range(KI):
            T(lambda e, k=k: e.matmul(pUY[0][0:DS, 0:1], wB32[:, k * DS:(k + 1) * DS],
                                      xc0[:, k:k + 1], start=(k == 0), stop=(k == KI - 1)),
              waits=[t_xc0, t_gld, bwait(("UY", 0))] if k == 0 else (),
              inc=SM["spt"].up() if k == KI - 1 else None)
        t_u0 = SM["spt"].tok()
        V(lambda e: e.tensor_scalar_mul(h1[:], pUY[0][0:DS, 0:1], 1.0 - DECAY),
          waits=[t_u0], inc=SM["spv"].up())
        t_h1 = SM["spv"].tok()
        bset(("UY", 0), t_h1)
        t_wc = DMA("sync", lambda e: e.dma_start(out=stg[0][0:DS, 0:DI], in_=d_wC),
                   waits=[bwait(("stg", 0))], sem=SM["sga"])
        for q in range(4):
            T(lambda e, q=q: e.matmul(RB[q][0:1, 0:512], h1[:],
                                      stg[0][0:DS, q * 512:(q + 1) * 512],
                                      start=True, stop=True),
              waits=[t_wc if q == 0 else None, t_h1 if q == 0 else None,
                     bwait(("A", q))],
              inc=SM["spt"].up() if q == 3 else None)
        t_ysp = SM["spt"].tok()
        bset(("stg", 0), t_ysp)
        for q in range(4):
            V(lambda e, q=q: e.tensor_copy(row_buf[0:1, q * 512:(q + 1) * 512],
                                           RB[q][0:1, 0:512]),
              waits=[t_ysp, bwait(("rowb", 0))] if q == 0 else (),
              inc=SM["spv"].up() if q == 3 else None)
        t_ysr = SM["spv"].tok()
        for q in range(4):
            bset(("A", q), t_ysr)
        for m in range(KI):
            T(lambda e, m=m: e.transpose(pUY[1][:, m:m + 1],
                                         row_buf[0:1, m * 128:(m + 1) * 128],
                                         ones_r[0:1, 0:1]),
              waits=[t_ysr, bwait(("UY", 1))] if m == 0 else (),
              inc=SM["spt"].up() if m == KI - 1 else None)
        t_ys0 = SM["spt"].tok()
        bset(("rowb", 0), t_ys0)
        V(lambda e: e.tensor_tensor(y20[:], pUY[1][:, 0:KI], tm0[:], op=ALU.mult),
          waits=[t_ys0, t_sz0], inc=SM["spv"].up())
        t_y20 = SM["spv"].tok()
        bset(("UY", 1), t_y20)

        plan_T(0)
        plan_T(1)

        for k in range(KI):
            sidx = (1 + k) % 4
            t_st = DMA(SLOTQ[sidx], lambda e, k=k, sidx=sidx: e.dma_start(
                out=stg[sidx][:, 0:D], in_=d_wout[k * 128:(k + 1) * 128, :]),
                waits=[bwait(("stg", sidx))], sem=SM[SLOTS[sidx]])
            C(lambda e, k=k, sidx=sidx: e.copy(w_out_bf[:, k * D:(k + 1) * D], stg[sidx][:, 0:D]),
              waits=[t_st], inc=SM["wcvt"].up())
            t_cvt = SM["wcvt"].tok()
            for q in range(2):
                T(lambda e, k=k, q=q, sidx=sidx: e.matmul(
                    RB[q][0:1, 0:512], y20[:, k:k + 1],
                    stg[sidx][:, q * 512:(q + 1) * 512],
                    start=(k == 0), stop=(k == KI - 1)),
                  waits=[t_st if q == 0 else None,
                         t_y20 if (k == 0 and q == 0) else None,
                         bwait(("A", q)) if k == 0 else None],
                  inc=SM["spt"].up() if q == 1 else None)
            bset(("stg", sidx), [SM["spt"].tok(), t_cvt])
        t_o1p = SM["spt"].tok()
        for q in range(2):
            V(lambda e, q=q: e.tensor_copy(row_buf[0:1, q * 512:(q + 1) * 512],
                                           RB[q][0:1, 0:512]),
              waits=[t_o1p, bwait(("rowb", 0))] if q == 0 else (),
              inc=SM["spv"].up() if q == 1 else None)
        t_o10r = SM["spv"].tok()
        bset(("A", 0), t_o10r)
        bset(("A", 1), t_o10r)
        for m in range(KD):
            T(lambda e, m=m: e.transpose(pUY[0][:, 32 + m:32 + m + 1],
                                         row_buf[0:1, m * 128:(m + 1) * 128],
                                         ones_r[0:1, 0:1]),
              waits=[t_o10r, bwait(("UY", 0))] if m == 0 else (),
              inc=SM["spt"].up() if m == KD - 1 else None)
        t_o10 = SM["spt"].tok()
        bset(("rowb", 0), t_o10)
        t_woutcvt = SM["wcvt"].tok()

        V(lambda e: e.tensor_tensor(xt0[:], pUY[0][:, 32:32 + KD], x0f[:], op=ALU.add),
          waits=[t_o10], inc=SM["spv"].up())
        t_xt0 = SM["spv"].tok()
        bset(("UY", 0), t_xt0)
        C(lambda e: e.activation(xt0sq[:], xt0[:], AF.Square), waits=[t_xt0],
          inc=SM["sps"].up())
        t_x2s = SM["sps"].tok()
        T(lambda e: e.matmul(pUY[1][0:1, 96:96 + KD], ones_c[:], xt0[:], start=True, stop=True),
          waits=[t_xt0, bwait(("UY", 1))])
        T(lambda e: e.matmul(pUY[1][0:1, 96 + KD:96 + 2 * KD], ones_c[:], xt0sq[:], start=True, stop=True),
          waits=[t_x2s], inc=SM["spt"].up())
        t_sums = SM["spt"].tok()
        V(lambda e: e.reduce_sum(rowm[:, 0:1], pUY[1][0:1, 96:96 + KD], axis=AX.X), waits=[t_sums])
        V(lambda e: e.reduce_sum(row4[:, 0:1], pUY[1][0:1, 96 + KD:96 + 2 * KD], axis=AX.X),
          inc=SM["spv"].up())
        bset(("UY", 1), SM["spv"].tok())
        V(lambda e: e.tensor_scalar_mul(rowm[:, 0:1], rowm[:, 0:1], 1.0 / D))
        V(lambda e: e.tensor_scalar_mul(row4[:, 0:1], row4[:, 0:1], 1.0 / D))
        V(lambda e: e.tensor_tensor(row4[:, 1:2], rowm[:, 0:1], rowm[:, 0:1], op=ALU.mult))
        V(lambda e: e.tensor_tensor(row4[:, 0:1], row4[:, 0:1], row4[:, 1:2], op=ALU.subtract))
        V(lambda e: e.tensor_scalar_add(row4[:, 0:1], row4[:, 0:1], EPS),
          inc=SM["spv"].up())
        t_v2 = SM["spv"].tok()
        C(lambda e: e.activation(row4[:, 1:2], row4[:, 0:1], AF.Sqrt), waits=[t_v2],
          inc=SM["sps"].up())
        t_sq2 = SM["sps"].tok()
        V(lambda e: e.reciprocal(rowm[:, 1:2], row4[:, 1:2]), waits=[t_sq2],
          inc=SM["spv"].up())
        t_ms2 = SM["spv"].tok()
        T(lambda e: e.matmul(pUY[1][:, 0:2], ones_r[:], rowm[:], start=True, stop=True),
          waits=[t_ms2, bwait(("UY", 1))], inc=SM["spt"].up())
        t_bcm = SM["spt"].tok()
        V(lambda e: e.tensor_copy(bc2[:], pUY[1][:, 0:2]), waits=[t_bcm],
          inc=SM["spv"].up())
        bset(("UY", 1), SM["spv"].tok())
        V(lambda e: e.tensor_scalar(ln20[:], xt0[:], bc2[:, 0:1], bc2[:, 1:2],
                                    op0=ALU.subtract, op1=ALU.mult), inc=SM["spv"].up())
        t_ln20 = SM["spv"].tok()
        for k in range(KD):
            T(lambda e, k=k: e.matmul(pUY[0][0:NE, 0:1], Wr32[:, k * NE:(k + 1) * NE],
                                      ln20[:, k:k + 1], start=(k == 0), stop=(k == KD - 1)),
              waits=[t_ln20, t_gld, bwait(("UY", 0))] if k == 0 else (),
              inc=SM["spt"].up() if k == KD - 1 else None)
        t_scp = SM["spt"].tok()
        V(lambda e: e.tensor_copy(h1[:, 0:1], pUY[0][0:DS, 0:1]), waits=[t_scp],
          inc=SM["spv"].up())
        t_scc = SM["spv"].tok()
        bset(("UY", 0), t_scc)
        T(lambda e: e.transpose(pUY[0][0:1, 32:32 + DS], h1[:, 0:1], id32[0:DS, 0:DS]),
          waits=[t_scc], inc=SM["spt"].up())
        t_sct = SM["spt"].tok()
        V(lambda e: e.tensor_tensor(sc_row[:], pUY[0][0:1, 32:32 + NE], br_s[:], op=ALU.add),
          waits=[t_sct, t_gld], inc=SM["spv"].up())
        bset(("UY", 0), SM["spv"].tok())
        V(lambda e: e.max(out=mx8[:], in_=sc_row[:]), waits=[t_iota])
        for j in range(2):
            V(lambda e, j=j: e.tensor_scalar(msk8[:], sc_row[:], mx8[:, j:j + 1], None,
                                             op0=ALU.is_equal))
            V(lambda e, j=j: e.tensor_copy(oh_bc[0:1, j * NE:(j + 1) * NE], msk8[:]))
            V(lambda e, j=j: e.tensor_tensor(msk8[:], msk8[:], iot8[:], op=ALU.mult))
            V(lambda e, j=j: e.reduce_sum(idxf[:, j:j + 1], msk8[:], axis=AX.X))
        P.ops["vector"][-1] = (P.ops["vector"][-1][0], P.ops["vector"][-1][1],
                               SM["spv"].up(), True)
        t_idx = SM["spv"].tok()
        T(lambda e: e.matmul(pUY[1][:, 0:2], ones_r[:], idxf[:], start=True, stop=True),
          waits=[t_idx, bwait(("UY", 1))])
        T(lambda e: e.matmul(pUY[1][:, 16:16 + 2 * NE], ones_r[:], oh_bc[0:1, :],
                             start=True, stop=True), inc=SM["spt"].up())
        t_ebc = SM["spt"].tok()
        V(lambda e: e.tensor_copy(ebc[:], pUY[1][:, 0:2]), waits=[t_ebc])
        V(lambda e: e.tensor_copy(oh_bc[:], pUY[1][:, 16:16 + 2 * NE]),
          inc=SM["spv"].up())
        bset(("UY", 1), SM["spv"].tok())
        V(lambda e: e.tensor_scalar(ebase[:], ebc[:], D, None, op0=ALU.mult))
        V(lambda e: e.tensor_tensor(ebase[:], ebase[:], iotp[:].to_broadcast([128, 2]),
                                    op=ALU.add))
        for slot in range(2):
            for kk in range(KD):
                V(lambda e, slot=slot, kk=kk: e.tensor_scalar(
                    gidx[:, slot * KD + kk:slot * KD + kk + 1],
                    ebase[:, slot:slot + 1], kk * 128, None, op0=ALU.add))
        P.ops["vector"][-1] = (P.ops["vector"][-1][0], P.ops["vector"][-1][1],
                               SM["route"].up(), True)
        t_route = SM["route"].tok()
        stg_done = []
        for q in range(4):
            g = bwait(("stg", q))
            if isinstance(g, list):
                stg_done.extend(g)
            elif g is not None:
                stg_done.append(g)
        guard[("xpregion", 0)] = stg_done
        guard[("sz", 0)] = stg_done + [t_o10]
        guard[("xcs", 0)] = [t_route]

    p1 = ExitStack()
    p1sb = lambda name, shape, dt: p1.enter_context(nc.sbuf_tensor(name, shape, dt))
    xp_buf = p1sb("xp_buf", [128, 2 * KI * (T1 + 3)], BF16)
    sz_buf = p1sb("sz_buf", [128, 2 * KI * T1], BF16)
    xcs = p1sb("xcs", [128, KI * T1], BF16)
    H_s = p1sb("H_s", [DS, T1], F32)
    H_bf = p1sb("H_bf", [128, T1], BF16)
    o1_s = p1sb("o1_s", [128, KD * T1], BF16)
    V(lambda e: e.memset(H_bf[:], 0.0))

    def plan_M1(c):
        if c >= NC1:
            return
        par = c % 2
        fm0 = par * KD * T1
        xp0 = par * KI * XPW
        sz0 = par * KI * T1
        for m in range(M1G):
            b = m % 4
            for k in range(KD):
                waits = [(SM["wcvt"], (k + 1) if m < KI else (KD + k + 1))] if c == 0 else []
                if k == 0:
                    waits.append(bwait(("A", b)))
                    if m < 2:
                        waits.append(tTe_tok[c])
                T(lambda e, m=m, k=k, b=b, fm0=fm0: e.matmul(
                    PB[b][:, 0:T1],
                    w_in_bf[:, k * 2 * DI + m * 128: k * 2 * DI + (m + 1) * 128],
                    ln1_fm[:, fm0 + k * T1: fm0 + (k + 1) * T1],
                    start=(k == 0), stop=(k == KD - 1)),
                  waits=waits,
                  inc=SM["m1"].up() if k == KD - 1 else None)
            tm = SM["m1"].tok()
            if m < KI:
                C(lambda e, m=m, b=b, xp0=xp0: e.copy(
                    xp_buf[:, xp0 + m * XPW + 3: xp0 + m * XPW + 3 + T1], PB[b][:, 0:T1]),
                  waits=[tm, guard.get(("xpregion", par))] if m == 0 else [tm],
                  inc=SM["m1ev"].up())
                bset(("A", b), SM["m1ev"].tok())
            else:
                C(lambda e, m=m, b=b, sz0=sz0: e.activation(
                    sz_buf[:, sz0 + (m - KI) * T1: sz0 + (m - KI + 1) * T1],
                    PB[b][:, 0:T1], AF.Silu),
                  waits=[tm, guard.get(("sz", par))] if m == KI else [tm],
                  inc=SM["m1es"].up())
                bset(("A", b), SM["m1es"].tok())
        m1_tok[c] = SM["m1"].tok()
        guard[("ln1_fm", par)] = m1_tok[c]

    def plan_conv_edges(c):
        if c >= NC1:
            return
        par = c % 2
        xp0 = par * KI * XPW
        xp1 = (1 - par) * KI * XPW
        t_xpev = SM["m1ev"].tok()
        for cc in range(KI):
            base = xp0 + cc * XPW
            if c == 0:
                V(lambda e, base=base: e.memset(xp_buf[:, base:base + 3], 0.0),
                  waits=[t_xpev] if cc == 0 else ())
            else:
                V(lambda e, base=base, cc=cc, xp1=xp1: e.tensor_copy(
                    xp_buf[:, base:base + 3],
                    xp_buf[:, xp1 + cc * XPW + T1: xp1 + cc * XPW + 3 + T1]),
                  waits=[t_xpev] if cc == 0 else (), drain=False)
        P.ops["vector"][-1] = (P.ops["vector"][-1][0], P.ops["vector"][-1][1],
                               SM["conv"].up(), True)
        edge_tok[c] = SM["conv"].tok()

    def plan_conv(c):
        if c >= NC1:
            return
        par = c % 2
        xp0 = par * KI * XPW
        t_edge = edge_tok[c]
        for cp in range(KI // 2):
            b = cp % 4
            for half in range(2):
                cc = cp * 2 + half
                base = xp0 + cc * XPW
                for j in range(4):
                    T(lambda e, base=base, cc=cc, j=j, b=b, half=half: e.matmul(
                        PB[b][:, half * T1:(half + 1) * T1],
                        diag_bf[:, (cc * 4 + j) * 128:(cc * 4 + j + 1) * 128],
                        xp_buf[:, base + j:base + j + T1],
                        start=(j == 0), stop=(j == 3)),
                      waits=[t_edge, t_gdg, bwait(("A", b)) if half == 0 else None] if (j == 0) else (),
                      inc=SM["m3"].up() if (j == 3 and half == 1) else None)
            tcv = SM["m3"].tok()
            for half in range(2):
                cc = cp * 2 + half
                C(lambda e, cc=cc, b=b, half=half: e.activation(
                    xcs[:, cc * T1:(cc + 1) * T1], PB[b][:, half * T1:(half + 1) * T1],
                    AF.Silu, bias=conv_b[:, cc:cc + 1], scale=1.0),
                  waits=[tcv, guard.get(("xcs", 0))] if (cp == 0 and half == 0) else ([tcv] if half == 0 else ()),
                  inc=SM["xc"].up())
            bset(("A", b), SM["xc"].tok())
        xc_tok[c] = SM["xc"].tok()

    def plan_tail_a(c):
        par = c % 2
        xc_base = SM["xc"].n - KI
        for k in range(KI):
            T(lambda e, k=k: e.matmul(
                pUY[0][0:DS, 0:T1], w_B_bf[:, k * DS:(k + 1) * DS],
                xcs[:, k * T1:(k + 1) * T1], start=(k == 0), stop=(k == KI - 1)),
              waits=[(SM["xc"], xc_base + k + 1),
                     bwait(("UY", 0)) if k == 0 else None, t_gld if k == 0 else None],
              inc=SM["m2"].up() if k == KI - 1 else None)
        m2_tok[c] = SM["m2"].tok()
        guard[("xcs", 0)] = m2_tok[c]
        V(lambda e: e.tensor_scalar_mul(H_s[:], pUY[0][0:DS, 0:T1], 1.0 - DECAY),
          waits=[m2_tok[c]], inc=SM["duv"].up())
        bset(("UY", 0), SM["duv"].tok())
        V(lambda e: e.tensor_tensor_scan(H_s[:], c09[:], H_s[:], hcarry[:],
                                         op0=ALU.mult, op1=ALU.add))
        V(lambda e: e.tensor_copy(hcarry[:], H_s[:, T1 - 1:T1]), drain=False)
        V(lambda e: e.tensor_copy(H_bf[0:DS, :], H_s[:]), inc=SM["Hv"].up(), drain=False)

    def plan_tail(c):
        par = c % 2
        xp0 = par * KI * XPW
        sz0 = par * KI * T1
        t_H = SM["Hv"].tok()

        t_zev = SM["m1es"].tok()
        y2_base = SM["y2"].n
        for mp in range(KI // 2):
            b = mp % 2
            for half in range(2):
                m = 2 * mp + half
                T(lambda e, m=m, b=b, half=half: e.matmul(
                    pUY[b][:, half * T1:(half + 1) * T1],
                    w_C_bf[:, m * 128:(m + 1) * 128], H_bf[:],
                    start=True, stop=True),
                  waits=[t_H if mp == 0 and half == 0 else None,
                         t_gld if mp == 0 and half == 0 else None,
                         bwait(("UY", b)) if half == 0 else None],
                  inc=SM["m3"].up() if half == 1 else None)
            tk = SM["m3"].tok()
            m0 = 2 * mp
            xv = xp_buf[:, xp0 + m0 * XPW: xp0 + (m0 + 2) * XPW].rearrange(
                "p (m w) -> p m w", w=XPW)[:, :, 0:T1]
            zv = sz_buf[:, sz0 + m0 * T1: sz0 + (m0 + 2) * T1].rearrange(
                "p (m w) -> p m w", w=T1)
            pv = pUY[b][:, 0:2 * T1].rearrange("p (m w) -> p m w", w=T1)
            V(lambda e, xv=xv, zv=zv, pv=pv: e.tensor_tensor(xv, pv, zv, op=ALU.mult),
              waits=[tk, t_zev] if mp == 0 else [tk],
              inc=SM["y2"].up(2), drain=False)
            bset(("UY", b), SM["y2"].tok())
        y2_tok[c] = SM["y2"].tok()
        guard[("sz", par)] = y2_tok[c]

        for m in range(KD):
            b = m % 4
            for k in range(KI):
                waits = [(SM["wcvt"], 2 * KD + k + 1)] if c == 0 else []
                if m == 0:
                    waits.append((SM["y2"], y2_base + 2 * (k // 2 + 1)))
                if k == 0:
                    waits.append(bwait(("A", b)))
                T(lambda e, m=m, k=k, b=b, xp0=xp0: e.matmul(
                    PB[b][:, 0:T1], w_out_bf[:, k * D + m * 128: k * D + (m + 1) * 128],
                    xp_buf[:, xp0 + k * XPW: xp0 + k * XPW + T1],
                    start=(k == 0), stop=(k == KI - 1)),
                  waits=waits,
                  inc=SM["m4"].up() if k == KI - 1 else None)
            tm = SM["m4"].tok()
            C(lambda e, m=m, b=b: e.copy(
                o1_s[:, m * T1:(m + 1) * T1], PB[b][:, 0:T1]),
              waits=[tm, guard.get(("o1", 0))] if m == 0 else [tm],
              inc=SM["o1e"].up())
            bset(("A", b), SM["o1e"].tok())
        m4_tok[c] = SM["m4"].tok()
        o1e_tok = SM["o1e"].tok()
        guard[("xpregion", par)] = m4_tok[c]

        slot3 = (c + 2) % 3
        gi = 0
        for tt in range(NT1):
            b = gi % 2
            for k in range(KD):
                T(lambda e, tt=tt, k=k, b=b: e.transpose(
                    pT16[b][:, k * 128:(k + 1) * 128],
                    o1_s[:, k * T1 + tt * 128: k * T1 + (tt + 1) * 128], id16[:]),
                  waits=[o1e_tok, bwait(("T16", b))] if k == 0 else (),
                  inc=SM["to1"].up() if k == KD - 1 else None)
            tk = SM["to1"].tok()
            V(lambda e, tt=tt, b=b, slot3=slot3: e.tensor_tensor(
                x_tm[:, (slot3 * NT1 + tt) * D:(slot3 * NT1 + tt + 1) * D],
                pT16[b][:, 0:D],
                x_tm[:, (slot3 * NT1 + tt) * D:(slot3 * NT1 + tt + 1) * D],
                op=ALU.add),
              waits=[tk],
              inc=SM["xte"].up(), drain=False)
            bset(("T16", b), SM["xte"].tok())
            gi += 1
            xte_tok[(c, tt)] = SM["xte"].tok()
        xte_tok[c] = SM["xte"].tok()
        guard[("o1", 0)] = SM["to1"].tok()
        late = c >= NC1 - 2
        xts = SM["xtst"] if late else SM["xtsA"]
        for tt in range(NT1):
            DMA("scalar" if late else "sync", lambda e, c=c, tt=tt, slot3=slot3: e.dma_start(
                out=d_xt[c * T1 + tt * 128: c * T1 + (tt + 1) * 128, :],
                in_=x_tm[:, (slot3 * NT1 + tt) * D:(slot3 * NT1 + tt + 1) * D]),
                waits=[xte_tok[(c, tt)]], sem=xts)
        guard[("x_tm", slot3)] = xts.tok()


    for slot in range(2):
        for kk in range(KD):
            V(lambda e, slot=slot, kk=kk: e.tensor_tensor(
                tmp8g[:], b1rl[:, kk * NE:(kk + 1) * NE],
                oh_bc[:, slot * NE:(slot + 1) * NE], op=ALU.mult),
              waits=[t_gld] if (slot == 0 and kk == 0) else ())
            V(lambda e, slot=slot, kk=kk: e.reduce_sum(
                b1g[:, slot * KD + kk:slot * KD + kk + 1], tmp8g[:], axis=AX.X))
    for slot in range(2):
        for kk in range(KD):
            dst = (b2sum if slot == 0 else b2g1)
            V(lambda e, slot=slot, kk=kk: e.tensor_tensor(
                tmp8g[:], b2rl[:, kk * NE:(kk + 1) * NE],
                oh_bc[:, slot * NE:(slot + 1) * NE], op=ALU.mult),
              waits=[t_gld] if (slot == 0 and kk == 0) else ())
            V(lambda e, dst=dst, kk=kk: e.reduce_sum(
                dst[:, kk:kk + 1], tmp8g[:], axis=AX.X))
    V(lambda e: e.tensor_tensor(b2sum[:], b2sum[:], b2g1[:], op=ALU.add))
    V(lambda e: e.tensor_scalar_mul(b2sum[:], b2sum[:], 0.5),
      inc=SM["spv"].up())
    t_bias = SM["spv"].tok()

    plan_M1(0)
    plan_conv_edges(0)
    plan_conv(0)
    for c in range(NC1):
        load_x(c + 2)
        plan_M1(c + 1)
        plan_norm(c + 2)
        plan_conv_edges(c + 1)
        plan_T(c + 2)
        plan_tail_a(c)
        plan_conv(c + 1)
        plan_tail(c)

    T(lambda e: e.matmul(pT32[0][0:1, 0:1], ones_r[0:1, 0:1], ones_r[0:1, 0:1],
                         start=True, stop=True),
      waits=[bwait(("A", 2))], inc=SM["p1dt"].up())
    V(lambda e: e.memset(st12[:, 0:1], 0.0), inc=SM["p1dv"].up())
    C(lambda e: e.memzero(csc[:]), inc=SM["p1ds"].up())
    t_p1d = [SM["p1dt"].tok(), SM["p1dv"].tok(), SM["p1ds"].tok()]
    p1.close()
    wsc.close()

    p2 = ExitStack()
    p2sb = lambda name, shape, dt: p2.enter_context(nc.sbuf_tensor(name, shape, dt))
    W1g = p2sb("W1g", [128, 2 * KD * D], BF16)
    W2g = p2sb("W2g", [128, 2 * KD * D], BF16)
    xt2 = p2sb("xt2", [128, 3 * NT2 * D], F32)
    ln2_tm = p2sb("ln2_tm", [128, NT2 * D], BF16)
    ln2_fm = p2sb("ln2_fm", [128, 2 * KD * T2], BF16)
    gel = p2sb("gel", [128, 2 * 2 * KD * T2], BF16)
    o_s = p2sb("o_s", [128, KD * T2], BF16)
    out_s = p2sb("out_s", [128, NT2 * D], F32)

    t_ph1w = [m4_tok[NC1 - 1], (SM["Hv"], NC1), (SM["xc"], NC1 * KI)]
    for b in range(4):
        bank_guard[("h", b)] = [SM["o1e"].tok(), (SM["xc"], NC1 * KI)]

    gw_tok = {}
    for nm, dd, Wg in (("1", d_W1, W1g), ("2", d_W2, W2g)):
        for slot in range(2):
            sem = SM["gw" + nm + ("a" if slot == 0 else "b")]
            for kk in range(KD):
                w = ([t_route, m1_tok[NC1 - 1]]
                     if (nm == "1" and slot == 0 and kk == 0) else ())
                DMA("gpsimd", lambda e, slot=slot, kk=kk, dd=dd, Wg=Wg: e.indirect_dma_start(
                    out=Wg[:, (slot * KD + kk) * D:(slot * KD + kk + 1) * D],
                    out_offset=None,
                    in_=dd.rearrange("e d f -> (e d) f"),
                    in_offset=bass.IndirectOffsetOnAxis(
                        ap=gidx[:, slot * KD + kk:slot * KD + kk + 1], axis=0)),
                    waits=w, sem=sem)
            for kk in range(KD):
                gw_tok[("W" + nm, slot, kk)] = sem.tok()

    xt2_tok = {}
    xt2_tile_tok = {}
    norm2_tok = {}
    norm2_base = {}
    tT2e_tok = {}
    m5_tok = {}
    m6_tok = {}
    gel_base = {}
    guard2 = {}

    def load_xt2(c):
        if c >= NC2:
            return
        par = c % 3
        toks = []
        for tt in range(NT2):
            toks.append(DMA("sync", lambda e, c=c, tt=tt, par=par: e.dma_start(
                out=xt2[:, (par * NT2 + tt) * D:(par * NT2 + tt + 1) * D],
                in_=d_xt[c * T2 + tt * 128: c * T2 + (tt + 1) * 128, :]),
                waits=[(SM["xtsA"], SM["xtsA"].n),
                       (SM["xtst"], SM["xtst"].n) if c == NC2 - 1 else None,
                       guard2.get(("xt2", par)),
                       ((m4_tok[NC1 - 1] if c < 2 else t_ph1w)
                        if (c < 3 and tt == 0) else None)],
                sem=SM["x2" + "abc"[par] + str(tt)]))
        xt2_tok[c] = toks[-1]
        xt2_tile_tok[c] = toks

    def plan_norm2(c):
        if c >= NC2:
            return
        par = c % 3
        base = par * NT2 * D
        norm2_base[c] = SM["norm2"].n
        for tt in range(NT2):
            V(lambda e, tt=tt, base=base: e.bn_stats(
                st12[:, 0:6], xt2[:, base + tt * D: base + tt * D + 512]),
              waits=[xt2_tile_tok[c][tt], guard2.get(("ln2_tm", 0)) if tt == 0 else None,
                     ([m4_tok[NC1 - 1], (SM["xtsA"], SM["xtsA"].n)]
                      if (c == 0 and tt == 0) else None)])
            V(lambda e, tt=tt, base=base: e.bn_stats(
                st12[:, 6:12], xt2[:, base + tt * D + 512: base + (tt + 1) * D]))
            V(lambda e: e.bn_aggr(mv[:], st12[:]))
            V(lambda e, tt=tt: e.tensor_copy(mAll[:, tt:tt + 1], mv[:, 0:1]))
            V(lambda e, tt=tt: e.tensor_scalar_add(vAll[:, tt:tt + 1], mv[:, 1:2], EPS))
        NT = NT2
        V(lambda e, NT=NT: e.tensor_scalar(rsA[:, 0:NT].bitcast(I32),
                                           vAll[:, 0:NT].bitcast(I32), 1,
                                           None, op0=ALU.arith_shift_right))
        V(lambda e, NT=NT: e.tensor_scalar(rsA[:, 0:NT].bitcast(I32),
                                           rsA[:, 0:NT].bitcast(I32), -1,
                                           0x5f3759df, op0=ALU.mult, op1=ALU.add))
        for _ in range(3):
            V(lambda e, NT=NT: e.tensor_tensor(nwt[:, 0:NT], rsA[:, 0:NT], rsA[:, 0:NT], op=ALU.mult))
            V(lambda e, NT=NT: e.tensor_tensor(nwt[:, 0:NT], nwt[:, 0:NT], vAll[:, 0:NT], op=ALU.mult))
            V(lambda e, NT=NT: e.tensor_scalar(nwt[:, 0:NT], nwt[:, 0:NT], -0.5, 1.5,
                                               op0=ALU.mult, op1=ALU.add))
            V(lambda e, NT=NT: e.tensor_tensor(rsA[:, 0:NT], rsA[:, 0:NT], nwt[:, 0:NT], op=ALU.mult))
        for tt in range(NT2):
            V(lambda e, tt=tt, base=base: e.tensor_scalar(
                ln2_tm[:, tt * D:(tt + 1) * D],
                xt2[:, base + tt * D: base + (tt + 1) * D],
                mAll[:, tt:tt + 1], rsA[:, tt:tt + 1],
                op0=ALU.subtract, op1=ALU.mult),
              inc=SM["norm2"].up(), drain=False)
        norm2_tok[c] = SM["norm2"].tok()

    def plan_T2(c):
        if c >= NC2:
            return
        par = c % 2
        fm0 = par * KD * T2
        for tt in range(NT2):
            b = tt % 2
            for k in range(KD):
                T(lambda e, tt=tt, k=k, b=b: e.transpose(
                    pT16[b][:, k * 128:(k + 1) * 128],
                    ln2_tm[:, tt * D + k * 128: tt * D + (k + 1) * 128], id16[:]),
                  waits=[(SM["norm2"], norm2_base[c] + tt + 1), bwait(("T16", b))] if k == 0 else (),
                  inc=SM["tT2"].up() if k == KD - 1 else None)
            tk = SM["tT2"].tok()
            dst = ln2_fm[:, fm0:fm0 + KD * T2].rearrange(
                "p (k t) -> p k t", t=T2)[:, :, tt * 128:(tt + 1) * 128]
            srcv = pT16[b][:, 0:KD * 128].rearrange("p (k t) -> p k t", t=128)
            V(lambda e, dst=dst, srcv=srcv: e.tensor_copy(dst, srcv),
              waits=[tk, guard2.get(("ln2_fm", par)) if tt == 0 else None,
                     ([(SM["xtsA"], SM["xtsA"].n), (SM["xtst"], SM["xtst"].n)]
                      if (c == 0 and tt == 0) else None)],
              inc=SM["tT2e"].up(), drain=False)
            bset(("T16", b), SM["tT2e"].tok())
        tT2e_tok[c] = SM["tT2e"].tok()
        guard2[("ln2_tm", 0)] = SM["tT2"].tok()

    def plan_M5(c):
        if c >= NC2:
            return
        par = c % 2
        fm0 = par * KD * T2
        g0 = par * 2 * KD * T2
        gel_base[c] = SM["gel"].n
        for slot in range(2):
            for m in range(KD):
                b = m % 4
                for k in range(KD):
                    waits = [gw_tok[("W1", slot, k)] if m == 0 else None]
                    if k == 0:
                        waits += [bwait(("h", b)),
                                  tT2e_tok[c] if (slot == 0 and m == 0) else None]
                    T(lambda e, slot=slot, m=m, k=k, b=b, fm0=fm0: e.matmul(
                        PB[b][:, 0:T2],
                        W1g[:, (slot * KD + k) * D + m * 128:(slot * KD + k) * D + (m + 1) * 128],
                        ln2_fm[:, fm0 + k * T2: fm0 + (k + 1) * T2],
                        start=(k == 0), stop=(k == KD - 1)),
                      waits=waits,
                      inc=SM["m5"].up() if k == KD - 1 else None)
                tm = SM["m5"].tok()
                C(lambda e, slot=slot, m=m, b=b, g0=g0: e.activation(
                    gel[:, g0 + (slot * KD + m) * T2: g0 + (slot * KD + m + 1) * T2],
                    PB[b][:, 0:T2],
                    AF.Gelu, bias=b1g[:, slot * KD + m:slot * KD + m + 1], scale=1.0),
                  waits=([tm, t_bias, guard2.get(("gel", par))]
                         + ([m4_tok[NC1 - 1], (SM["xtst"], SM["xtst"].n)] if c < 2 else [])
                         ) if (slot == 0 and m == 0) else [tm],
                  inc=SM["gel"].up())
                bset(("h", b), SM["gel"].tok())
        m5_tok[c] = SM["m5"].tok()
        guard2[("ln2_fm", par)] = m5_tok[c]

    def plan_M6(c):
        par = c % 2
        g0 = par * 2 * KD * T2
        for m2 in range(KD):
            b = m2 % 2
            for i, (slot, kk) in enumerate([(s, k) for s in range(2) for k in range(KD)]):
                waits = [gw_tok[("W2", slot, kk)] if m2 == 0 else None,
                         (SM["gel"], gel_base[c] + slot * KD + kk + 1) if m2 == 0 else None]
                if i == 0:
                    waits.append(bwait(("o", b)))
                T(lambda e, m2=m2, slot=slot, kk=kk, b=b, i=i, g0=g0: e.matmul(
                    pUY[b][:, 0:T2],
                    W2g[:, (slot * KD + kk) * D + m2 * 128:(slot * KD + kk) * D + (m2 + 1) * 128],
                    gel[:, g0 + (slot * KD + kk) * T2: g0 + (slot * KD + kk + 1) * T2],
                    start=(i == 0), stop=(i == 2 * KD - 1)),
                  waits=waits,
                  inc=SM["m6"].up() if i == 2 * KD - 1 else None)
            tm = SM["m6"].tok()
            C(lambda e, m2=m2, b=b: e.activation(
                o_s[:, m2 * T2:(m2 + 1) * T2], pUY[b][:, 0:T2], AF.Identity,
                bias=b2sum[:, m2:m2 + 1], scale=0.5),
              waits=([tm, guard2.get(("o_s", 0))]
                     + ([m4_tok[NC1 - 1]] if c == 0 else [])) if m2 == 0 else [tm],
              inc=SM["oev"].up())
            bset(("o", b), SM["oev"].tok())
        m6_tok[c] = SM["m6"].tok()
        guard2[("gel", par)] = m6_tok[c]

    def plan_To2(c):
        par = c % 3
        t_oev = SM["oev"].tok()
        gi = 0
        for tt in range(NT2):
            b = tt % 2
            for k in range(KD):
                T(lambda e, tt=tt, k=k, b=b: e.transpose(
                    pT16[b][:, k * 128:(k + 1) * 128],
                    o_s[:, k * T2 + tt * 128: k * T2 + (tt + 1) * 128], id16[:]),
                  waits=[t_oev, bwait(("T16", b))] if k == 0 else (),
                  inc=SM["to2"].up() if k == KD - 1 else None)
            tk = SM["to2"].tok()
            V(lambda e, tt=tt, b=b, par=par: e.tensor_tensor(
                out_s[:, tt * D:(tt + 1) * D], pT16[b][:, 0:D],
                xt2[:, (par * NT2 + tt) * D:(par * NT2 + tt + 1) * D],
                op=ALU.add),
              waits=[tk, guard2.get(("out", 0)) if gi == 0 else None,
                     ([m4_tok[NC1 - 1], y2_tok[NC1 - 1]] if (c == 0 and gi == 0) else None)],
              inc=SM["oute"].up(), drain=False)
            bset(("T16", b), SM["oute"].tok())
            gi += 1
            tout_t = SM["oute"].tok()
            DMA("sync", lambda e, c=c, tt=tt: e.dma_start(
                out=d_y[c * T2 + tt * 128: c * T2 + (tt + 1) * 128, :],
                in_=out_s[:, tt * D:(tt + 1) * D]),
                waits=[tout_t], sem=SM["yst"])
        guard2[("out", 0)] = SM["yst"].tok()
        guard2[("o_s", 0)] = SM["to2"].tok()
        guard2[("xt2", par)] = SM["oute"].tok()

    load_xt2(0)
    plan_norm2(0)
    plan_T2(0)
    plan_M5(0)
    load_xt2(1)
    plan_norm2(1)
    plan_T2(1)
    for c in range(NC2):
        load_xt2(c + 2)
        plan_norm2(c + 2)
        plan_M5(c + 1)
        plan_T2(c + 2)
        plan_M6(c)
        plan_To2(c)

    P.add("sync", lambda e: e.nop(), [(SM["yst"], SM["yst"].n)], None, False)

    with nc.Block() as blk:
        @blk.sync
        def _(e):
            P.replay("sync", e)

        @blk.tensor
        def _(e):
            P.replay("tensor", e)

        @blk.vector
        def _(e):
            P.replay("vector", e)

        @blk.scalar
        def _(e):
            P.replay("scalar", e)

        @blk.gpsimd
        def _(e):
            P.replay("gpsimd", e)

    p2.close()
    es.close()
    return nc


def make_core_inputs(xb, d, S):
    conv_w = np.asarray(d["L0_attn_conv_w"], np.float32)
    conv_fm = conv_w[:, 0, :].reshape(KI, 128, 4).transpose(1, 0, 2).reshape(128, KI * 4)
    convb_fm = np.asarray(d["L0_attn_conv_b"], np.float32).reshape(KI, 128).T
    b1rl = np.asarray(d["L0_ffn_b1"], np.float32).reshape(NE, KD, 128).transpose(2, 1, 0).reshape(128, KD * NE)
    b2rl = np.asarray(d["L0_ffn_b2"], np.float32).reshape(NE, KD, 128).transpose(2, 1, 0).reshape(128, KD * NE)
    return {
        "x": np.ascontiguousarray(xb[:S]),
        "w_in": np.ascontiguousarray(np.asarray(d["L0_attn_w_in"], np.float32)),
        "w_out": np.ascontiguousarray(np.asarray(d["L0_attn_w_out"], np.float32)),
        "w_B": np.ascontiguousarray(np.asarray(d["L0_attn_w_B"], np.float32)),
        "w_C": np.ascontiguousarray(np.asarray(d["L0_attn_w_C"], np.float32)),
        "conv_fm": np.ascontiguousarray(conv_fm),
        "convb_fm": np.ascontiguousarray(convb_fm),
        "Wr": np.ascontiguousarray(np.asarray(d["L0_ffn_Wr"], np.float32)),
        "br": np.ascontiguousarray(np.asarray(d["L0_ffn_br"], np.float32).reshape(1, NE)),
        "W1": np.ascontiguousarray(np.asarray(d["L0_ffn_W1"], np.float32)),
        "W2": np.ascontiguousarray(np.asarray(d["L0_ffn_W2"], np.float32)),
        "b1rl": np.ascontiguousarray(b1rl),
        "b2rl": np.ascontiguousarray(b2rl),
        "id32": np.eye(DS, dtype=np.float32),
        "id16": np.eye(128, dtype=bf16),
    }


def kernel(**inputs):
    x = np.asarray(inputs["L0_attn_x"], np.float32)
    B, S, _ = x.shape
    nc = build_program(S=S)
    in_maps = [make_core_inputs(x[b], inputs, S) for b in range(B)]
    res = run_bass_kernel_spmd(nc, in_maps, core_ids=list(range(B)))
    return np.stack([np.asarray(res.results[b]["y"], np.float32) for b in range(B)], axis=0)


# version 50
# speedup vs baseline: 1.0041x; 1.0041x over previous
import numpy as np
import ml_dtypes
from contextlib import ExitStack

import concourse.bass as bass
import concourse.mybir as mybir
from concourse.bass_utils import run_bass_kernel_spmd

F32 = mybir.dt.float32
BF16 = mybir.dt.bfloat16
I32 = mybir.dt.int32
AF = mybir.ActivationFunctionType
ALU = mybir.AluOpType
AX = mybir.AxisListType
bf16 = ml_dtypes.bfloat16

D = 1024
DI = 2048
DS = 16
KD = D // 128
KI = DI // 128
NE = 8
EPS = 1e-5
DECAY = 0.9
ENGS = ("sync", "tensor", "vector", "scalar", "gpsimd")


class Sem:
    __slots__ = ("h", "n", "name")

    def __init__(self, h, name):
        self.h = h
        self.n = 0
        self.name = name

    def bump(self, k):
        self.n += k
        return (self, self.n)

    def up(self, k=1):
        self.n += k
        return (self, k)

    def tok(self):
        return (self, self.n)


class Plan:
    def __init__(self):
        self.ops = {e: [] for e in ENGS}

    def add(self, eng, fn, waits=(), inc=None, drain=False):
        ws = []
        for w in waits:
            if w is None:
                continue
            if isinstance(w, list):
                ws.extend(x for x in w if x is not None)
            else:
                ws.append(w)
        self.ops[eng].append((fn, ws, inc, drain))

    def replay(self, eng_name, e):
        for fn, waits, inc, drain in self.ops[eng_name]:
            for s, v in waits:
                e.wait_ge(s.h, v)
            ins = fn(e)
            if inc is not None:
                ins.then_inc(inc[0].h, inc[1])
            if drain:
                e.drain()


def build_program(S=4096, T1=256, T2=512):
    NC1 = S // T1
    NC2 = S // T2
    NT1 = T1 // 128
    NT2 = T2 // 128
    M1G = 4 * KD

    nc = bass.Bass("TRN2", target_bir_lowering=False, debug=False)

    d_x = nc.dram_tensor("x", [S, D], F32, kind="ExternalInput").ap()
    d_win = nc.dram_tensor("w_in", [D, 2 * DI], F32, kind="ExternalInput").ap()
    d_wout = nc.dram_tensor("w_out", [DI, D], F32, kind="ExternalInput").ap()
    d_wB = nc.dram_tensor("w_B", [DI, DS], F32, kind="ExternalInput").ap()
    d_wC = nc.dram_tensor("w_C", [DS, DI], F32, kind="ExternalInput").ap()
    d_convw = nc.dram_tensor("conv_fm", [128, KI * 4], F32, kind="ExternalInput").ap()
    d_convb = nc.dram_tensor("convb_fm", [128, KI], F32, kind="ExternalInput").ap()
    d_Wr = nc.dram_tensor("Wr", [D, NE], F32, kind="ExternalInput").ap()
    d_br = nc.dram_tensor("br", [1, NE], F32, kind="ExternalInput").ap()
    d_W1 = nc.dram_tensor("W1", [NE, D, D], F32, kind="ExternalInput").ap()
    d_W2 = nc.dram_tensor("W2", [NE, D, D], F32, kind="ExternalInput").ap()
    d_b1rl = nc.dram_tensor("b1rl", [128, KD * NE], F32, kind="ExternalInput").ap()
    d_b2rl = nc.dram_tensor("b2rl", [128, KD * NE], F32, kind="ExternalInput").ap()
    d_id32 = nc.dram_tensor("id32", [DS, DS], F32, kind="ExternalInput").ap()
    d_id16 = nc.dram_tensor("id16", [128, 128], BF16, kind="ExternalInput").ap()
    d_xt = nc.dram_tensor("xt_scr", [S, D], F32, kind="Internal").ap()
    d_y = nc.dram_tensor("y", [S, D], F32, kind="ExternalOutput").ap()

    P = Plan()
    es = ExitStack()
    sem_names = [
        "sga", "sgb", "sgc", "sgd", "sge", "sgf", "gld", "gdg", "gid", "wcz", "gio", "wcvt", "x0", "xa", "xb", "xcq", "spt", "spv", "sps", "route",
        "norm", "tT", "tTe", "m1", "m1ev", "m1es",
        "conv", "xc", "m2", "duv", "Hv", "m3", "y2", "m4", "o1e",
        "to1", "xte", "xtst", "xtsA", "p1dt", "p1dv", "p1ds",
        "gw1a", "gw1b", "gw2a", "gw2b", "x2a0", "x2a1", "x2a2", "x2a3", "x2b0", "x2b1", "x2b2", "x2b3", "x2c0", "x2c1", "x2c2", "x2c3", "norm2", "tT2", "tT2e",
        "m5", "gel", "m6", "oev", "to2", "oute", "yst",
    ]
    SM = {n: Sem(es.enter_context(nc.semaphore(name="sem_" + n)), n) for n in sem_names}

    def T(fn, waits=(), inc=None):
        P.add("tensor", fn, waits, inc, drain=False)

    def V(fn, waits=(), inc=None, drain=True):
        P.add("vector", fn, waits, inc, drain=drain)

    def C(fn, waits=(), inc=None, drain=True):
        P.add("scalar", fn, waits, inc, drain=drain)

    def G(fn, waits=(), inc=None, drain=True):
        P.add("gpsimd", fn, waits, inc, drain=drain)

    def DMA(eng, fn, waits=(), sem=None):
        tok = sem.bump(16)
        P.add(eng, fn, waits, (sem, 16), drain=False)
        return tok

    sb = lambda name, shape, dt: es.enter_context(nc.sbuf_tensor(name, shape, dt))
    b1rl = sb("b1rl_s", [128, KD * NE], F32)
    b2rl = sb("b2rl_s", [128, KD * NE], F32)
    id32 = sb("id32_s", [DS, DS], F32)
    id16 = sb("id16_s", [128, 128], BF16)
    br_s = sb("br_s", [1, NE], F32)
    iot8 = sb("iot8", [1, NE], F32)
    iotp = sb("iotp", [128, 1], I32)
    ones_r = sb("ones_r", [1, 128], F32)
    ones_c = sb("ones_c", [128, 1], F32)
    gidx = sb("gidx", [128, 2 * KD], I32)
    oh_bc = sb("oh_bc", [128, 2 * NE], F32)
    b1g = sb("b1g", [128, 2 * KD], F32)
    b2sum = sb("b2sum", [128, KD], F32)
    b2g1 = sb("b2g1", [128, KD], F32)
    st12 = sb("st12", [128, 12], F32)
    mAll = sb("mAll", [128, 4], F32)
    vAll = sb("vAll", [128, 4], F32)
    rsA = sb("rsA", [128, 4], F32)
    nwt = sb("nwt", [128, 4], F32)
    tmp8g = sb("tmp8g", [128, NE], F32)
    mv = sb("mv", [128, 2], F32)
    csc = sb("csc", [1, 1], F32)

    wsc = ExitStack()
    wsb = lambda name, shape, dt: wsc.enter_context(nc.sbuf_tensor(name, shape, dt))
    w_in_bf = wsb("w_in_bf", [128, KD * 2 * DI], BF16)
    w_out_bf = wsb("w_out_bf", [128, KI * D], BF16)
    w_B_bf = wsb("w_B_bf", [128, KI * DS], BF16)
    w_C_bf = wsb("w_C_bf", [128, DI], BF16)
    conv_w = wsb("conv_w", [128, KI * 4], F32)
    conv_b = wsb("conv_b", [128, KI], F32)
    hcarry = wsb("hcarry", [DS, 1], F32)
    c09 = wsb("c09", [DS, T1], F32)
    diag_bf = wsb("diag_bf", [128, 4 * KI * 128], BF16)
    x_tm = wsb("x_tm", [128, 3 * NT1 * D], F32)
    ln1_tm = wsb("ln1_tm", [128, 2 * NT1 * D], BF16)
    ln1_fm = wsb("ln1_fm", [128, 2 * KD * T1], BF16)

    ps = lambda name, shape, dt: es.enter_context(nc.psum_tensor(name, shape, dt))
    pA = [ps("pA0", [128, 512], F32), ps("pA1", [128, 512], F32)]
    pT16 = [ps("pT16a", [128, 1024], BF16), ps("pT16b", [128, 1024], BF16)]
    pT32 = [ps("pT32a", [128, 512], F32), ps("pT32b", [128, 512], F32)]
    pUY = [ps("pUY0", [128, 512], F32), ps("pUY1", [128, 512], F32)]
    PB = [pA[0], pA[1], pT32[0], pT32[1]]
    bank_guard = {}

    def bwait(bank_id):
        return bank_guard.get(bank_id)

    def bset(bank_id, tok):
        bank_guard[bank_id] = tok

    t_id32 = DMA("gpsimd", lambda e: e.dma_start(out=id32[:], in_=d_id32), sem=SM["gid"])
    t_id16 = DMA("gpsimd", lambda e: e.dma_start(out=id16[:], in_=d_id16), sem=SM["gid"])
    t_gid = SM["gid"].tok()
    t_convw = DMA("gpsimd", lambda e: e.dma_start(out=conv_w[:], in_=d_convw), sem=SM["gld"])
    t_convb = DMA("gpsimd", lambda e: e.dma_start(out=conv_b[:], in_=d_convb), sem=SM["gld"])
    t_b1rl = DMA("gpsimd", lambda e: e.dma_start(out=b1rl[:], in_=d_b1rl), sem=SM["gld"])
    t_b2rl = DMA("gpsimd", lambda e: e.dma_start(out=b2rl[:], in_=d_b2rl), sem=SM["gld"])
    t_br = DMA("gpsimd", lambda e: e.dma_start(out=br_s[:], in_=d_br), sem=SM["gld"])
    t_wBbf = None
    for k in range(KI):
        t_wBbf = DMA("gpsimd", lambda e, k=k: e.dma_start(
            out=w_B_bf[:, k * DS:(k + 1) * DS], in_=d_wB[k * 128:(k + 1) * 128, :]),
            sem=SM["gld"])
    V(lambda e: e.memset(w_C_bf[:], 0.0), inc=SM["wcz"].up())
    t_wCbf = DMA("gpsimd", lambda e: e.dma_start(out=w_C_bf[0:DS, :], in_=d_wC),
                 waits=[SM["wcz"].tok()], sem=SM["gld"])

    G(lambda e: e.iota(iot8[:], pattern=[[1, NE]], base=0, channel_multiplier=0,
                       allow_small_or_imprecise_dtypes=True), inc=SM["gio"].up())
    G(lambda e: e.iota(iotp[:], pattern=[[0, 1]], base=0, channel_multiplier=1),
      inc=SM["gio"].up())
    t_iota = SM["gio"].tok()
    V(lambda e: e.memset(ones_r[:], 1.0))
    V(lambda e: e.memset(ones_c[:], 1.0))
    V(lambda e: e.memset(c09[:], DECAY))
    V(lambda e: e.memset(hcarry[:], 0.0), inc=SM["spv"].up())
    t_memset = SM["spv"].tok()

    XPW = T1 + 3
    x_tok = {}
    x_tile_tok = {}
    guard = {}
    norm_tok = {}
    tTe_tok = {}
    m1_tok = {}
    edge_tok = {}
    m2_tok = {}
    m4_tok = {}
    y2_tok = {}
    xc_tok = {}
    xte_tok = {}

    def load_x(c):
        if c >= NC1:
            return
        par = (c + 2) % 3
        toks = []
        for tt in range(NT1):
            toks.append(DMA("sync", lambda e, c=c, tt=tt, par=par: e.dma_start(
                out=x_tm[:, (par * NT1 + tt) * D:(par * NT1 + tt + 1) * D],
                in_=d_x[c * T1 + tt * 128: c * T1 + (tt + 1) * 128, :]),
                waits=[guard.get(("x_tm", par))], sem=SM[("xa", "xb", "xcq")[par]]))
        x_tok[c] = toks[-1]
        x_tile_tok[c] = toks

    def plan_norm(c):
        if c >= NC1:
            return
        par = (c + 2) % 3
        base = par * NT1 * D
        lt0 = (c % 2) * NT1 * D
        for tt in range(NT1):
            V(lambda e, tt=tt, base=base: e.bn_stats(
                st12[:, 0:6], x_tm[:, base + tt * D: base + tt * D + 512]),
              waits=[x_tok[c] if tt == 0 else None,
                     guard.get(("ln1_tm", c % 2)) if tt == 0 else None])
            V(lambda e, tt=tt, base=base: e.bn_stats(
                st12[:, 6:12], x_tm[:, base + tt * D + 512: base + (tt + 1) * D]))
            V(lambda e: e.bn_aggr(mv[:], st12[:]))
            V(lambda e, tt=tt: e.tensor_copy(mAll[:, tt:tt + 1], mv[:, 0:1]))
            V(lambda e, tt=tt: e.tensor_scalar_add(vAll[:, tt:tt + 1], mv[:, 1:2], EPS))
        NT = NT1
        V(lambda e, NT=NT: e.tensor_scalar(rsA[:, 0:NT].bitcast(I32),
                                           vAll[:, 0:NT].bitcast(I32), 1,
                                           None, op0=ALU.arith_shift_right))
        V(lambda e, NT=NT: e.tensor_scalar(rsA[:, 0:NT].bitcast(I32),
                                           rsA[:, 0:NT].bitcast(I32), -1,
                                           0x5f3759df, op0=ALU.mult, op1=ALU.add))
        for _ in range(3):
            V(lambda e, NT=NT: e.tensor_tensor(nwt[:, 0:NT], rsA[:, 0:NT], rsA[:, 0:NT], op=ALU.mult))
            V(lambda e, NT=NT: e.tensor_tensor(nwt[:, 0:NT], nwt[:, 0:NT], vAll[:, 0:NT], op=ALU.mult))
            V(lambda e, NT=NT: e.tensor_scalar(nwt[:, 0:NT], nwt[:, 0:NT], -0.5, 1.5,
                                               op0=ALU.mult, op1=ALU.add))
            V(lambda e, NT=NT: e.tensor_tensor(rsA[:, 0:NT], rsA[:, 0:NT], nwt[:, 0:NT], op=ALU.mult))
        for tt in range(NT1):
            V(lambda e, tt=tt, base=base, lt0=lt0: e.tensor_scalar(
                ln1_tm[:, lt0 + tt * D: lt0 + (tt + 1) * D],
                x_tm[:, base + tt * D: base + (tt + 1) * D],
                mAll[:, tt:tt + 1], rsA[:, tt:tt + 1],
                op0=ALU.subtract, op1=ALU.mult),
              inc=SM["norm"].up(), drain=False)
        norm_tok[c] = SM["norm"].tok()

    def plan_T(c):
        if c >= NC1:
            return
        par = c % 2
        fm0 = par * KD * T1
        lt0 = par * NT1 * D
        for tt in range(NT1):
            b = tt % 2
            for k in range(KD):
                T(lambda e, tt=tt, k=k, b=b, lt0=lt0: e.transpose(
                    pT16[b][:, k * 128:(k + 1) * 128],
                    ln1_tm[:, lt0 + tt * D + k * 128: lt0 + tt * D + (k + 1) * 128], id16[:]),
                  waits=[norm_tok[c], t_gid, bwait(("T16", b))] if k == 0 else (),
                  inc=SM["tT"].up() if k == KD - 1 else None)
            tk = SM["tT"].tok()
            dst = ln1_fm[:, fm0:fm0 + KD * T1].rearrange(
                "p (k t) -> p k t", t=T1)[:, :, tt * 128:(tt + 1) * 128]
            srcv = pT16[b][:, 0:KD * 128].rearrange("p (k t) -> p k t", t=128)
            V(lambda e, dst=dst, srcv=srcv: e.tensor_copy(dst, srcv),
              waits=[tk, guard.get(("ln1_fm", par)) if tt == 0 else None],
              inc=SM["tTe"].up(), drain=False)
            bset(("T16", b), SM["tTe"].tok())
        tTe_tok[c] = SM["tTe"].tok()
        guard[("ln1_tm", par)] = SM["tT"].tok()


    with ExitStack() as side:
        ssb = lambda name, shape, dt: side.enter_context(nc.sbuf_tensor(name, shape, dt))
        stg = [ssb("stg%d" % i, [128, DI], F32) for i in range(4)]
        SLOTQ = ("sync", "sync", "gpsimd", "gpsimd")
        SLOTS = ("sga", "sgb", "sgc", "sgd")
        row_buf = ssb("row_buf", [1, 2048], F32)
        wB32 = ssb("wB32", [128, KI * DS], F32)
        Wr32 = ssb("Wr32", [128, KD * NE], F32)
        row4 = ssb("row4", [1, 8], F32)
        rowm = ssb("rowm", [1, 2], F32)
        bc2 = ssb("bc2", [128, 2], F32)
        x0f = ssb("x0f", [128, KD], F32)
        ln0 = ssb("ln0", [128, KD], F32)
        xc0 = ssb("xc0", [128, KI], F32)
        tm0 = ssb("tm0", [128, KI], F32)
        h1 = ssb("h1", [DS, 1], F32)
        y20 = ssb("y20", [128, KI], F32)
        xt0 = ssb("xt0", [128, KD], F32)
        xt0sq = ssb("xt0sq", [128, KD], F32)
        ln20 = ssb("ln20", [128, KD], F32)
        sc_row = ssb("sc_row", [1, NE], F32)
        mx8 = ssb("mx8", [1, NE], F32)
        msk8 = ssb("msk8", [1, NE], F32)
        idxf = ssb("idxf", [1, 2], F32)
        ebc = ssb("ebc", [128, 2], I32)
        ebase = ssb("ebase", [128, 2], I32)
        tmp8 = ssb("tmp8", [128, NE], F32)
        x0r = row_buf[0:1, 0:D]

        for k in range(KI):
            t_wB32 = DMA("gpsimd", lambda e, k=k: e.dma_start(
                out=wB32[:, k * DS:(k + 1) * DS], in_=d_wB[k * 128:(k + 1) * 128, :]),
                sem=SM["gld"])
        t_Wr = None
        for k in range(KD):
            t_Wr = DMA("gpsimd", lambda e, k=k: e.dma_start(
                out=Wr32[:, k * NE:(k + 1) * NE], in_=d_Wr[k * 128:(k + 1) * 128, :]),
                sem=SM["gld"])
        t_x0 = DMA("sync", lambda e: e.dma_start(out=x0r[:], in_=d_x[0:1, :]), sem=SM["x0"])
        t_gld = SM["gld"].tok()
        load_x(0)
        load_x(1)

        V(lambda e: e.bn_stats(st12[0:1, 0:6], x0r[0:1, 0:512]), waits=[t_x0])
        V(lambda e: e.bn_stats(st12[0:1, 6:12], x0r[0:1, 512:1024]))
        V(lambda e: e.bn_aggr(mv[0:1, :], st12[0:1, :]))
        V(lambda e: e.tensor_scalar_add(row4[0:1, 0:1], mv[0:1, 1:2], EPS))
        V(lambda e: e.tensor_scalar(row4[0:1, 1:2].bitcast(I32),
                                    row4[0:1, 0:1].bitcast(I32), 1, None,
                                    op0=ALU.arith_shift_right))
        V(lambda e: e.tensor_scalar(row4[0:1, 1:2].bitcast(I32),
                                    row4[0:1, 1:2].bitcast(I32), -1,
                                    0x5f3759df, op0=ALU.mult, op1=ALU.add))
        for _ in range(3):
            V(lambda e: e.tensor_tensor(row4[0:1, 2:3], row4[0:1, 1:2], row4[0:1, 1:2], op=ALU.mult))
            V(lambda e: e.tensor_tensor(row4[0:1, 2:3], row4[0:1, 2:3], row4[0:1, 0:1], op=ALU.mult))
            V(lambda e: e.tensor_scalar(row4[0:1, 2:3], row4[0:1, 2:3], -0.5, 1.5,
                                        op0=ALU.mult, op1=ALU.add))
            V(lambda e: e.tensor_tensor(row4[0:1, 1:2], row4[0:1, 1:2], row4[0:1, 2:3], op=ALU.mult))
        V(lambda e: e.tensor_scalar(row_buf[0:1, D:2 * D], x0r[:], mv[0:1, 0:1],
                                    row4[0:1, 1:2], op0=ALU.subtract, op1=ALU.mult),
          inc=SM["spv"].up())
        t_lnrow = SM["spv"].tok()
        for k in range(KD):
            T(lambda e, k=k: e.transpose(pUY[0][:, k:k + 1],
                                         row_buf[0:1, D + k * 128: D + (k + 1) * 128],
                                         ones_r[0:1, 0:1]),
              waits=[t_lnrow, t_gid, t_memset] if k == 0 else (), inc=SM["spt"].up())
        t_lnt = SM["spt"].tok()
        V(lambda e: e.tensor_copy(ln0[:], pUY[0][:, 0:KD]), waits=[t_lnt],
          inc=SM["spv"].up())
        t_ln0 = SM["spv"].tok()
        bset(("UY", 0), t_ln0)
        for k in range(KD):
            T(lambda e, k=k: e.transpose(pUY[1][:, 8 + k:8 + k + 1], x0r[0:1, k * 128:(k + 1) * 128],
                                         ones_r[0:1, 0:1]),
              waits=[t_x0, t_gid, t_memset] if k == 0 else (), inc=SM["spt"].up())
        t_x0t = SM["spt"].tok()
        V(lambda e: e.tensor_copy(x0f[:], pUY[1][:, 8:8 + KD]), waits=[t_x0t],
          inc=SM["spv"].up())
        t_x0f = SM["spv"].tok()
        bset(("UY", 1), t_x0f)

        plan_norm(0)
        plan_norm(1)
        for t in range(4 * KI):
            V(lambda e, t=t: e.tensor_scalar(
                diag_bf[:, t * 128:(t + 1) * 128], id16[:], conv_w[:, t:t + 1],
                None, op0=ALU.mult),
              waits=[t_gld, t_gid] if t == 0 else (),
              inc=SM["gdg"].up() if t == 4 * KI - 1 else None)
        t_gdg = SM["gdg"].tok()

        RB = [pA[0], pA[1], pT32[0], pT32[1]]
        for hf in range(2):
            for k in range(KD):
                idx = hf * KD + k
                sidx = idx % 4
                t_st = DMA(SLOTQ[sidx], lambda e, k=k, hf=hf, sidx=sidx: e.dma_start(
                    out=stg[sidx][:], in_=d_win[k * 128:(k + 1) * 128, hf * DI:(hf + 1) * DI]),
                    waits=[bwait(("stg", sidx))], sem=SM[SLOTS[sidx]])
                C(lambda e, k=k, hf=hf, sidx=sidx: e.copy(
                    w_in_bf[:, k * 2 * DI + hf * DI: k * 2 * DI + (hf + 1) * DI], stg[sidx][:]),
                  waits=[t_st], inc=SM["wcvt"].up(), drain=False)
                t_cvt = SM["wcvt"].tok()
                for q in range(4):
                    T(lambda e, k=k, q=q, sidx=sidx: e.matmul(
                        RB[q][0:1, 0:512], ln0[:, k:k + 1],
                        stg[sidx][:, q * 512:(q + 1) * 512],
                        start=(k == 0), stop=(k == KD - 1)),
                      waits=[t_st if q == 0 else None,
                             t_ln0 if (idx == 0 and q == 0) else None,
                             bwait(("A", q)) if k == 0 else None],
                      inc=SM["spt"].up() if q == 3 else None)
                bset(("stg", sidx), [SM["spt"].tok(), t_cvt])
            tmm = SM["spt"].tok()
            for q in range(4):
                V(lambda e, q=q: e.tensor_copy(row_buf[0:1, q * 512:(q + 1) * 512],
                                               RB[q][0:1, 0:512]),
                  waits=[tmm, bwait(("rowb", 0))] if q == 0 else (),
                  inc=SM["spv"].up() if q == 3 else None)
            t_rowe = SM["spv"].tok()
            for q in range(4):
                bset(("A", q), t_rowe)
            for m in range(KI):
                T(lambda e, m=m, hf=hf: e.transpose(
                    pUY[1][:, 64 + hf * KI + m: 64 + hf * KI + m + 1],
                    row_buf[0:1, m * 128:(m + 1) * 128], ones_r[0:1, 0:1]),
                  waits=([t_rowe] + ([bwait(("UY", 1))] if hf == 0 else [])) if m == 0 else (),
                  inc=SM["spt"].up() if m == KI - 1 else None)
            bset(("rowb", 0), SM["spt"].tok())
        t_xz0 = SM["spt"].tok()
        t_wincvt = SM["wcvt"].tok()

        V(lambda e: e.tensor_tensor(tm0[:], pUY[1][:, 64:64 + KI],
                                    conv_w[:, 3::4], op=ALU.mult),
          waits=[t_xz0, t_gld])
        V(lambda e: e.tensor_tensor(tm0[:], tm0[:], conv_b[:, 0:KI], op=ALU.add),
          waits=[t_gld], inc=SM["spv"].up())
        t_c0 = SM["spv"].tok()
        C(lambda e: e.activation(xc0[:], tm0[:], AF.Silu),
          waits=[t_c0], inc=SM["sps"].up())
        t_xc0 = SM["sps"].tok()
        C(lambda e: e.activation(tm0[:], pUY[1][:, 64 + KI:64 + 2 * KI], AF.Silu),
          waits=[t_xz0], inc=SM["sps"].up())
        t_sz0 = SM["sps"].tok()
        bset(("UY", 1), [t_c0, t_sz0])
        for k in range(KI):
            T(lambda e, k=k: e.matmul(pUY[0][0:DS, 0:1], wB32[:, k * DS:(k + 1) * DS],
                                      xc0[:, k:k + 1], start=(k == 0), stop=(k == KI - 1)),
              waits=[t_xc0, t_gld, bwait(("UY", 0))] if k == 0 else (),
              inc=SM["spt"].up() if k == KI - 1 else None)
        t_u0 = SM["spt"].tok()
        V(lambda e: e.tensor_scalar_mul(h1[:], pUY[0][0:DS, 0:1], 1.0 - DECAY),
          waits=[t_u0], inc=SM["spv"].up())
        t_h1 = SM["spv"].tok()
        bset(("UY", 0), t_h1)
        t_wc = DMA("sync", lambda e: e.dma_start(out=stg[0][0:DS, 0:DI], in_=d_wC),
                   waits=[bwait(("stg", 0))], sem=SM["sga"])
        for q in range(4):
            T(lambda e, q=q: e.matmul(RB[q][0:1, 0:512], h1[:],
                                      stg[0][0:DS, q * 512:(q + 1) * 512],
                                      start=True, stop=True),
              waits=[t_wc if q == 0 else None, t_h1 if q == 0 else None,
                     bwait(("A", q))],
              inc=SM["spt"].up() if q == 3 else None)
        t_ysp = SM["spt"].tok()
        bset(("stg", 0), t_ysp)
        for q in range(4):
            V(lambda e, q=q: e.tensor_copy(row_buf[0:1, q * 512:(q + 1) * 512],
                                           RB[q][0:1, 0:512]),
              waits=[t_ysp, bwait(("rowb", 0))] if q == 0 else (),
              inc=SM["spv"].up() if q == 3 else None)
        t_ysr = SM["spv"].tok()
        for q in range(4):
            bset(("A", q), t_ysr)
        for m in range(KI):
            T(lambda e, m=m: e.transpose(pUY[1][:, m:m + 1],
                                         row_buf[0:1, m * 128:(m + 1) * 128],
                                         ones_r[0:1, 0:1]),
              waits=[t_ysr, bwait(("UY", 1))] if m == 0 else (),
              inc=SM["spt"].up() if m == KI - 1 else None)
        t_ys0 = SM["spt"].tok()
        bset(("rowb", 0), t_ys0)
        V(lambda e: e.tensor_tensor(y20[:], pUY[1][:, 0:KI], tm0[:], op=ALU.mult),
          waits=[t_ys0, t_sz0], inc=SM["spv"].up())
        t_y20 = SM["spv"].tok()
        bset(("UY", 1), t_y20)

        plan_T(0)
        plan_T(1)

        for k in range(KI):
            sidx = (1 + k) % 4
            t_st = DMA(SLOTQ[sidx], lambda e, k=k, sidx=sidx: e.dma_start(
                out=stg[sidx][:, 0:D], in_=d_wout[k * 128:(k + 1) * 128, :]),
                waits=[bwait(("stg", sidx))], sem=SM[SLOTS[sidx]])
            C(lambda e, k=k, sidx=sidx: e.copy(w_out_bf[:, k * D:(k + 1) * D], stg[sidx][:, 0:D]),
              waits=[t_st], inc=SM["wcvt"].up(), drain=False)
            t_cvt = SM["wcvt"].tok()
            for q in range(2):
                T(lambda e, k=k, q=q, sidx=sidx: e.matmul(
                    RB[q][0:1, 0:512], y20[:, k:k + 1],
                    stg[sidx][:, q * 512:(q + 1) * 512],
                    start=(k == 0), stop=(k == KI - 1)),
                  waits=[t_st if q == 0 else None,
                         t_y20 if (k == 0 and q == 0) else None,
                         bwait(("A", q)) if k == 0 else None],
                  inc=SM["spt"].up() if q == 1 else None)
            bset(("stg", sidx), [SM["spt"].tok(), t_cvt])
        t_o1p = SM["spt"].tok()
        for q in range(2):
            V(lambda e, q=q: e.tensor_copy(row_buf[0:1, q * 512:(q + 1) * 512],
                                           RB[q][0:1, 0:512]),
              waits=[t_o1p, bwait(("rowb", 0))] if q == 0 else (),
              inc=SM["spv"].up() if q == 1 else None)
        t_o10r = SM["spv"].tok()
        bset(("A", 0), t_o10r)
        bset(("A", 1), t_o10r)
        for m in range(KD):
            T(lambda e, m=m: e.transpose(pUY[0][:, 32 + m:32 + m + 1],
                                         row_buf[0:1, m * 128:(m + 1) * 128],
                                         ones_r[0:1, 0:1]),
              waits=[t_o10r, bwait(("UY", 0))] if m == 0 else (),
              inc=SM["spt"].up() if m == KD - 1 else None)
        t_o10 = SM["spt"].tok()
        bset(("rowb", 0), t_o10)
        t_woutcvt = SM["wcvt"].tok()

        V(lambda e: e.tensor_tensor(xt0[:], pUY[0][:, 32:32 + KD], x0f[:], op=ALU.add),
          waits=[t_o10], inc=SM["spv"].up())
        t_xt0 = SM["spv"].tok()
        bset(("UY", 0), t_xt0)
        C(lambda e: e.activation(xt0sq[:], xt0[:], AF.Square), waits=[t_xt0],
          inc=SM["sps"].up())
        t_x2s = SM["sps"].tok()
        T(lambda e: e.matmul(pUY[1][0:1, 96:96 + KD], ones_c[:], xt0[:], start=True, stop=True),
          waits=[t_xt0, bwait(("UY", 1))])
        T(lambda e: e.matmul(pUY[1][0:1, 96 + KD:96 + 2 * KD], ones_c[:], xt0sq[:], start=True, stop=True),
          waits=[t_x2s], inc=SM["spt"].up())
        t_sums = SM["spt"].tok()
        V(lambda e: e.reduce_sum(rowm[:, 0:1], pUY[1][0:1, 96:96 + KD], axis=AX.X), waits=[t_sums])
        V(lambda e: e.reduce_sum(row4[:, 0:1], pUY[1][0:1, 96 + KD:96 + 2 * KD], axis=AX.X),
          inc=SM["spv"].up())
        bset(("UY", 1), SM["spv"].tok())
        V(lambda e: e.tensor_scalar_mul(rowm[:, 0:1], rowm[:, 0:1], 1.0 / D))
        V(lambda e: e.tensor_scalar_mul(row4[:, 0:1], row4[:, 0:1], 1.0 / D))
        V(lambda e: e.tensor_tensor(row4[:, 1:2], rowm[:, 0:1], rowm[:, 0:1], op=ALU.mult))
        V(lambda e: e.tensor_tensor(row4[:, 0:1], row4[:, 0:1], row4[:, 1:2], op=ALU.subtract))
        V(lambda e: e.tensor_scalar_add(row4[:, 0:1], row4[:, 0:1], EPS),
          inc=SM["spv"].up())
        t_v2 = SM["spv"].tok()
        C(lambda e: e.activation(row4[:, 1:2], row4[:, 0:1], AF.Sqrt), waits=[t_v2],
          inc=SM["sps"].up())
        t_sq2 = SM["sps"].tok()
        V(lambda e: e.reciprocal(rowm[:, 1:2], row4[:, 1:2]), waits=[t_sq2],
          inc=SM["spv"].up())
        t_ms2 = SM["spv"].tok()
        T(lambda e: e.matmul(pUY[1][:, 0:2], ones_r[:], rowm[:], start=True, stop=True),
          waits=[t_ms2, bwait(("UY", 1))], inc=SM["spt"].up())
        t_bcm = SM["spt"].tok()
        V(lambda e: e.tensor_copy(bc2[:], pUY[1][:, 0:2]), waits=[t_bcm],
          inc=SM["spv"].up())
        bset(("UY", 1), SM["spv"].tok())
        V(lambda e: e.tensor_scalar(ln20[:], xt0[:], bc2[:, 0:1], bc2[:, 1:2],
                                    op0=ALU.subtract, op1=ALU.mult), inc=SM["spv"].up())
        t_ln20 = SM["spv"].tok()
        for k in range(KD):
            T(lambda e, k=k: e.matmul(pUY[0][0:NE, 0:1], Wr32[:, k * NE:(k + 1) * NE],
                                      ln20[:, k:k + 1], start=(k == 0), stop=(k == KD - 1)),
              waits=[t_ln20, t_gld, bwait(("UY", 0))] if k == 0 else (),
              inc=SM["spt"].up() if k == KD - 1 else None)
        t_scp = SM["spt"].tok()
        V(lambda e: e.tensor_copy(h1[:, 0:1], pUY[0][0:DS, 0:1]), waits=[t_scp],
          inc=SM["spv"].up())
        t_scc = SM["spv"].tok()
        bset(("UY", 0), t_scc)
        T(lambda e: e.transpose(pUY[0][0:1, 32:32 + DS], h1[:, 0:1], id32[0:DS, 0:DS]),
          waits=[t_scc], inc=SM["spt"].up())
        t_sct = SM["spt"].tok()
        V(lambda e: e.tensor_tensor(sc_row[:], pUY[0][0:1, 32:32 + NE], br_s[:], op=ALU.add),
          waits=[t_sct, t_gld], inc=SM["spv"].up())
        bset(("UY", 0), SM["spv"].tok())
        V(lambda e: e.max(out=mx8[:], in_=sc_row[:]), waits=[t_iota])
        for j in range(2):
            V(lambda e, j=j: e.tensor_scalar(msk8[:], sc_row[:], mx8[:, j:j + 1], None,
                                             op0=ALU.is_equal))
            V(lambda e, j=j: e.tensor_copy(oh_bc[0:1, j * NE:(j + 1) * NE], msk8[:]))
            V(lambda e, j=j: e.tensor_tensor(msk8[:], msk8[:], iot8[:], op=ALU.mult))
            V(lambda e, j=j: e.reduce_sum(idxf[:, j:j + 1], msk8[:], axis=AX.X))
        P.ops["vector"][-1] = (P.ops["vector"][-1][0], P.ops["vector"][-1][1],
                               SM["spv"].up(), True)
        t_idx = SM["spv"].tok()
        T(lambda e: e.matmul(pUY[1][:, 0:2], ones_r[:], idxf[:], start=True, stop=True),
          waits=[t_idx, bwait(("UY", 1))])
        T(lambda e: e.matmul(pUY[1][:, 16:16 + 2 * NE], ones_r[:], oh_bc[0:1, :],
                             start=True, stop=True), inc=SM["spt"].up())
        t_ebc = SM["spt"].tok()
        V(lambda e: e.tensor_copy(ebc[:], pUY[1][:, 0:2]), waits=[t_ebc])
        V(lambda e: e.tensor_copy(oh_bc[:], pUY[1][:, 16:16 + 2 * NE]),
          inc=SM["spv"].up())
        bset(("UY", 1), SM["spv"].tok())
        V(lambda e: e.tensor_scalar(ebase[:], ebc[:], D, None, op0=ALU.mult))
        V(lambda e: e.tensor_tensor(ebase[:], ebase[:], iotp[:].to_broadcast([128, 2]),
                                    op=ALU.add))
        for slot in range(2):
            for kk in range(KD):
                V(lambda e, slot=slot, kk=kk: e.tensor_scalar(
                    gidx[:, slot * KD + kk:slot * KD + kk + 1],
                    ebase[:, slot:slot + 1], kk * 128, None, op0=ALU.add))
        P.ops["vector"][-1] = (P.ops["vector"][-1][0], P.ops["vector"][-1][1],
                               SM["route"].up(), True)
        t_route = SM["route"].tok()
        stg_done = []
        for q in range(4):
            g = bwait(("stg", q))
            if isinstance(g, list):
                stg_done.extend(g)
            elif g is not None:
                stg_done.append(g)
        guard[("xpregion", 0)] = stg_done
        guard[("sz", 0)] = stg_done + [t_o10]
        guard[("xcs", 0)] = [t_route]

    p1 = ExitStack()
    p1sb = lambda name, shape, dt: p1.enter_context(nc.sbuf_tensor(name, shape, dt))
    xp_buf = p1sb("xp_buf", [128, 2 * KI * (T1 + 3)], BF16)
    sz_buf = p1sb("sz_buf", [128, 2 * KI * T1], BF16)
    xcs = p1sb("xcs", [128, KI * T1], BF16)
    H_s = p1sb("H_s", [DS, T1], F32)
    H_bf = p1sb("H_bf", [128, T1], BF16)
    o1_s = p1sb("o1_s", [128, KD * T1], BF16)
    V(lambda e: e.memset(H_bf[:], 0.0))

    def plan_M1(c):
        if c >= NC1:
            return
        par = c % 2
        fm0 = par * KD * T1
        xp0 = par * KI * XPW
        sz0 = par * KI * T1
        for m in range(M1G):
            b = m % 4
            for k in range(KD):
                waits = [(SM["wcvt"], (k + 1) if m < KI else (KD + k + 1))] if c == 0 else []
                if k == 0:
                    waits.append(bwait(("A", b)))
                    if m < 2:
                        waits.append(tTe_tok[c])
                T(lambda e, m=m, k=k, b=b, fm0=fm0: e.matmul(
                    PB[b][:, 0:T1],
                    w_in_bf[:, k * 2 * DI + m * 128: k * 2 * DI + (m + 1) * 128],
                    ln1_fm[:, fm0 + k * T1: fm0 + (k + 1) * T1],
                    start=(k == 0), stop=(k == KD - 1)),
                  waits=waits,
                  inc=SM["m1"].up() if k == KD - 1 else None)
            tm = SM["m1"].tok()
            if m < KI:
                C(lambda e, m=m, b=b, xp0=xp0: e.copy(
                    xp_buf[:, xp0 + m * XPW + 3: xp0 + m * XPW + 3 + T1], PB[b][:, 0:T1]),
                  waits=[tm, guard.get(("xpregion", par))] if m == 0 else [tm],
                  inc=SM["m1ev"].up(), drain=False)
                bset(("A", b), SM["m1ev"].tok())
            else:
                C(lambda e, m=m, b=b, sz0=sz0: e.activation(
                    sz_buf[:, sz0 + (m - KI) * T1: sz0 + (m - KI + 1) * T1],
                    PB[b][:, 0:T1], AF.Silu),
                  waits=[tm, guard.get(("sz", par))] if m == KI else [tm],
                  inc=SM["m1es"].up(), drain=False)
                bset(("A", b), SM["m1es"].tok())
        m1_tok[c] = SM["m1"].tok()
        guard[("ln1_fm", par)] = m1_tok[c]

    def plan_conv_edges(c):
        if c >= NC1:
            return
        par = c % 2
        xp0 = par * KI * XPW
        xp1 = (1 - par) * KI * XPW
        t_xpev = SM["m1ev"].tok()
        for cc in range(KI):
            base = xp0 + cc * XPW
            if c == 0:
                V(lambda e, base=base: e.memset(xp_buf[:, base:base + 3], 0.0),
                  waits=[t_xpev] if cc == 0 else ())
            else:
                V(lambda e, base=base, cc=cc, xp1=xp1: e.tensor_copy(
                    xp_buf[:, base:base + 3],
                    xp_buf[:, xp1 + cc * XPW + T1: xp1 + cc * XPW + 3 + T1]),
                  waits=[t_xpev] if cc == 0 else (), drain=False)
        P.ops["vector"][-1] = (P.ops["vector"][-1][0], P.ops["vector"][-1][1],
                               SM["conv"].up(), True)
        edge_tok[c] = SM["conv"].tok()

    def plan_conv(c):
        if c >= NC1:
            return
        par = c % 2
        xp0 = par * KI * XPW
        t_edge = edge_tok[c]
        for cp in range(KI // 2):
            b = cp % 4
            for half in range(2):
                cc = cp * 2 + half
                base = xp0 + cc * XPW
                for j in range(4):
                    T(lambda e, base=base, cc=cc, j=j, b=b, half=half: e.matmul(
                        PB[b][:, half * T1:(half + 1) * T1],
                        diag_bf[:, (cc * 4 + j) * 128:(cc * 4 + j + 1) * 128],
                        xp_buf[:, base + j:base + j + T1],
                        start=(j == 0), stop=(j == 3)),
                      waits=[t_edge, t_gdg, bwait(("A", b)) if half == 0 else None] if (j == 0) else (),
                      inc=SM["m3"].up() if (j == 3 and half == 1) else None)
            tcv = SM["m3"].tok()
            for half in range(2):
                cc = cp * 2 + half
                C(lambda e, cc=cc, b=b, half=half: e.activation(
                    xcs[:, cc * T1:(cc + 1) * T1], PB[b][:, half * T1:(half + 1) * T1],
                    AF.Silu, bias=conv_b[:, cc:cc + 1], scale=1.0),
                  waits=[tcv, guard.get(("xcs", 0))] if (cp == 0 and half == 0) else ([tcv] if half == 0 else ()),
                  inc=SM["xc"].up(), drain=False)
            bset(("A", b), SM["xc"].tok())
        xc_tok[c] = SM["xc"].tok()

    def plan_tail_a(c):
        par = c % 2
        xc_base = SM["xc"].n - KI
        for k in range(KI):
            T(lambda e, k=k: e.matmul(
                pUY[0][0:DS, 0:T1], w_B_bf[:, k * DS:(k + 1) * DS],
                xcs[:, k * T1:(k + 1) * T1], start=(k == 0), stop=(k == KI - 1)),
              waits=[(SM["xc"], xc_base + k + 1),
                     bwait(("UY", 0)) if k == 0 else None, t_gld if k == 0 else None],
              inc=SM["m2"].up() if k == KI - 1 else None)
        m2_tok[c] = SM["m2"].tok()
        guard[("xcs", 0)] = m2_tok[c]
        V(lambda e: e.tensor_scalar_mul(H_s[:], pUY[0][0:DS, 0:T1], 1.0 - DECAY),
          waits=[m2_tok[c]], inc=SM["duv"].up())
        bset(("UY", 0), SM["duv"].tok())
        V(lambda e: e.tensor_tensor_scan(H_s[:], c09[:], H_s[:], hcarry[:],
                                         op0=ALU.mult, op1=ALU.add))
        V(lambda e: e.tensor_copy(hcarry[:], H_s[:, T1 - 1:T1]), drain=False)
        V(lambda e: e.tensor_copy(H_bf[0:DS, :], H_s[:]), inc=SM["Hv"].up(), drain=False)

    def plan_tail(c):
        par = c % 2
        xp0 = par * KI * XPW
        sz0 = par * KI * T1
        t_H = SM["Hv"].tok()

        t_zev = SM["m1es"].tok()
        y2_base = SM["y2"].n
        for mp in range(KI // 2):
            b = mp % 2
            for half in range(2):
                m = 2 * mp + half
                T(lambda e, m=m, b=b, half=half: e.matmul(
                    pUY[b][:, half * T1:(half + 1) * T1],
                    w_C_bf[:, m * 128:(m + 1) * 128], H_bf[:],
                    start=True, stop=True),
                  waits=[t_H if mp == 0 and half == 0 else None,
                         t_gld if mp == 0 and half == 0 else None,
                         bwait(("UY", b)) if half == 0 else None],
                  inc=SM["m3"].up() if half == 1 else None)
            tk = SM["m3"].tok()
            m0 = 2 * mp
            xv = xp_buf[:, xp0 + m0 * XPW: xp0 + (m0 + 2) * XPW].rearrange(
                "p (m w) -> p m w", w=XPW)[:, :, 0:T1]
            zv = sz_buf[:, sz0 + m0 * T1: sz0 + (m0 + 2) * T1].rearrange(
                "p (m w) -> p m w", w=T1)
            pv = pUY[b][:, 0:2 * T1].rearrange("p (m w) -> p m w", w=T1)
            V(lambda e, xv=xv, zv=zv, pv=pv: e.tensor_tensor(xv, pv, zv, op=ALU.mult),
              waits=[tk, t_zev] if mp == 0 else [tk],
              inc=SM["y2"].up(2), drain=False)
            bset(("UY", b), SM["y2"].tok())
        y2_tok[c] = SM["y2"].tok()
        guard[("sz", par)] = y2_tok[c]

        for m in range(KD):
            b = m % 4
            for k in range(KI):
                waits = [(SM["wcvt"], 2 * KD + k + 1)] if c == 0 else []
                if m == 0:
                    waits.append((SM["y2"], y2_base + 2 * (k // 2 + 1)))
                if k == 0:
                    waits.append(bwait(("A", b)))
                T(lambda e, m=m, k=k, b=b, xp0=xp0: e.matmul(
                    PB[b][:, 0:T1], w_out_bf[:, k * D + m * 128: k * D + (m + 1) * 128],
                    xp_buf[:, xp0 + k * XPW: xp0 + k * XPW + T1],
                    start=(k == 0), stop=(k == KI - 1)),
                  waits=waits,
                  inc=SM["m4"].up() if k == KI - 1 else None)
            tm = SM["m4"].tok()
            C(lambda e, m=m, b=b: e.copy(
                o1_s[:, m * T1:(m + 1) * T1], PB[b][:, 0:T1]),
              waits=[tm, guard.get(("o1", 0))] if m == 0 else [tm],
              inc=SM["o1e"].up(), drain=False)
            bset(("A", b), SM["o1e"].tok())
        m4_tok[c] = SM["m4"].tok()
        o1e_tok = SM["o1e"].tok()
        guard[("xpregion", par)] = m4_tok[c]

        slot3 = (c + 2) % 3
        gi = 0
        for tt in range(NT1):
            b = gi % 2
            for k in range(KD):
                T(lambda e, tt=tt, k=k, b=b: e.transpose(
                    pT16[b][:, k * 128:(k + 1) * 128],
                    o1_s[:, k * T1 + tt * 128: k * T1 + (tt + 1) * 128], id16[:]),
                  waits=[o1e_tok, bwait(("T16", b))] if k == 0 else (),
                  inc=SM["to1"].up() if k == KD - 1 else None)
            tk = SM["to1"].tok()
            V(lambda e, tt=tt, b=b, slot3=slot3: e.tensor_tensor(
                x_tm[:, (slot3 * NT1 + tt) * D:(slot3 * NT1 + tt + 1) * D],
                pT16[b][:, 0:D],
                x_tm[:, (slot3 * NT1 + tt) * D:(slot3 * NT1 + tt + 1) * D],
                op=ALU.add),
              waits=[tk],
              inc=SM["xte"].up(), drain=False)
            bset(("T16", b), SM["xte"].tok())
            gi += 1
            xte_tok[(c, tt)] = SM["xte"].tok()
        xte_tok[c] = SM["xte"].tok()
        guard[("o1", 0)] = SM["to1"].tok()
        late = c >= NC1 - 2
        xts = SM["xtst"] if late else SM["xtsA"]
        for tt in range(NT1):
            DMA("scalar" if late else "sync", lambda e, c=c, tt=tt, slot3=slot3: e.dma_start(
                out=d_xt[c * T1 + tt * 128: c * T1 + (tt + 1) * 128, :],
                in_=x_tm[:, (slot3 * NT1 + tt) * D:(slot3 * NT1 + tt + 1) * D]),
                waits=[xte_tok[(c, tt)]], sem=xts)
        guard[("x_tm", slot3)] = xts.tok()


    for slot in range(2):
        for kk in range(KD):
            V(lambda e, slot=slot, kk=kk: e.tensor_tensor(
                tmp8g[:], b1rl[:, kk * NE:(kk + 1) * NE],
                oh_bc[:, slot * NE:(slot + 1) * NE], op=ALU.mult),
              waits=[t_gld] if (slot == 0 and kk == 0) else ())
            V(lambda e, slot=slot, kk=kk: e.reduce_sum(
                b1g[:, slot * KD + kk:slot * KD + kk + 1], tmp8g[:], axis=AX.X))
    for slot in range(2):
        for kk in range(KD):
            dst = (b2sum if slot == 0 else b2g1)
            V(lambda e, slot=slot, kk=kk: e.tensor_tensor(
                tmp8g[:], b2rl[:, kk * NE:(kk + 1) * NE],
                oh_bc[:, slot * NE:(slot + 1) * NE], op=ALU.mult),
              waits=[t_gld] if (slot == 0 and kk == 0) else ())
            V(lambda e, dst=dst, kk=kk: e.reduce_sum(
                dst[:, kk:kk + 1], tmp8g[:], axis=AX.X))
    V(lambda e: e.tensor_tensor(b2sum[:], b2sum[:], b2g1[:], op=ALU.add))
    V(lambda e: e.tensor_scalar_mul(b2sum[:], b2sum[:], 0.5),
      inc=SM["spv"].up())
    t_bias = SM["spv"].tok()

    plan_M1(0)
    plan_conv_edges(0)
    plan_conv(0)
    for c in range(NC1):
        load_x(c + 2)
        plan_M1(c + 1)
        plan_norm(c + 2)
        plan_conv_edges(c + 1)
        plan_T(c + 2)
        plan_tail_a(c)
        plan_conv(c + 1)
        plan_tail(c)

    T(lambda e: e.matmul(pT32[0][0:1, 0:1], ones_r[0:1, 0:1], ones_r[0:1, 0:1],
                         start=True, stop=True),
      waits=[bwait(("A", 2))], inc=SM["p1dt"].up())
    V(lambda e: e.memset(st12[:, 0:1], 0.0), inc=SM["p1dv"].up())
    C(lambda e: e.memzero(csc[:]), inc=SM["p1ds"].up())
    t_p1d = [SM["p1dt"].tok(), SM["p1dv"].tok(), SM["p1ds"].tok()]
    p1.close()
    wsc.close()

    p2 = ExitStack()
    p2sb = lambda name, shape, dt: p2.enter_context(nc.sbuf_tensor(name, shape, dt))
    W1g = p2sb("W1g", [128, 2 * KD * D], BF16)
    W2g = p2sb("W2g", [128, 2 * KD * D], BF16)
    xt2 = p2sb("xt2", [128, 3 * NT2 * D], F32)
    ln2_tm = p2sb("ln2_tm", [128, NT2 * D], BF16)
    ln2_fm = p2sb("ln2_fm", [128, 2 * KD * T2], BF16)
    gel = p2sb("gel", [128, 2 * 2 * KD * T2], BF16)
    o_s = p2sb("o_s", [128, KD * T2], BF16)
    out_s = p2sb("out_s", [128, NT2 * D], F32)

    t_ph1w = [m4_tok[NC1 - 1], (SM["Hv"], NC1), (SM["xc"], NC1 * KI)]
    for b in range(4):
        bank_guard[("h", b)] = [SM["o1e"].tok(), (SM["xc"], NC1 * KI)]

    gw_tok = {}
    for nm, dd, Wg in (("1", d_W1, W1g), ("2", d_W2, W2g)):
        for slot in range(2):
            sem = SM["gw" + nm + ("a" if slot == 0 else "b")]
            for kk in range(KD):
                w = ([t_route, m1_tok[NC1 - 1]]
                     if (nm == "1" and slot == 0 and kk == 0) else ())
                DMA("gpsimd", lambda e, slot=slot, kk=kk, dd=dd, Wg=Wg: e.indirect_dma_start(
                    out=Wg[:, (slot * KD + kk) * D:(slot * KD + kk + 1) * D],
                    out_offset=None,
                    in_=dd.rearrange("e d f -> (e d) f"),
                    in_offset=bass.IndirectOffsetOnAxis(
                        ap=gidx[:, slot * KD + kk:slot * KD + kk + 1], axis=0)),
                    waits=w, sem=sem)
            for kk in range(KD):
                gw_tok[("W" + nm, slot, kk)] = sem.tok()

    xt2_tok = {}
    xt2_tile_tok = {}
    norm2_tok = {}
    norm2_base = {}
    tT2e_tok = {}
    m5_tok = {}
    m6_tok = {}
    gel_base = {}
    guard2 = {}

    def load_xt2(c):
        if c >= NC2:
            return
        par = c % 3
        toks = []
        for tt in range(NT2):
            toks.append(DMA("sync", lambda e, c=c, tt=tt, par=par: e.dma_start(
                out=xt2[:, (par * NT2 + tt) * D:(par * NT2 + tt + 1) * D],
                in_=d_xt[c * T2 + tt * 128: c * T2 + (tt + 1) * 128, :]),
                waits=[(SM["xtsA"], SM["xtsA"].n),
                       (SM["xtst"], SM["xtst"].n) if c == NC2 - 1 else None,
                       guard2.get(("xt2", par)),
                       ((m4_tok[NC1 - 1] if c < 2 else t_ph1w)
                        if (c < 3 and tt == 0) else None)],
                sem=SM["x2" + "abc"[par] + str(tt)]))
        xt2_tok[c] = toks[-1]
        xt2_tile_tok[c] = toks

    def plan_norm2(c):
        if c >= NC2:
            return
        par = c % 3
        base = par * NT2 * D
        norm2_base[c] = SM["norm2"].n
        for tt in range(NT2):
            V(lambda e, tt=tt, base=base: e.bn_stats(
                st12[:, 0:6], xt2[:, base + tt * D: base + tt * D + 512]),
              waits=[xt2_tile_tok[c][tt], guard2.get(("ln2_tm", 0)) if tt == 0 else None,
                     ([m4_tok[NC1 - 1], (SM["xtsA"], SM["xtsA"].n)]
                      if (c == 0 and tt == 0) else None)])
            V(lambda e, tt=tt, base=base: e.bn_stats(
                st12[:, 6:12], xt2[:, base + tt * D + 512: base + (tt + 1) * D]))
            V(lambda e: e.bn_aggr(mv[:], st12[:]))
            V(lambda e, tt=tt: e.tensor_copy(mAll[:, tt:tt + 1], mv[:, 0:1]))
            V(lambda e, tt=tt: e.tensor_scalar_add(vAll[:, tt:tt + 1], mv[:, 1:2], EPS))
        NT = NT2
        V(lambda e, NT=NT: e.tensor_scalar(rsA[:, 0:NT].bitcast(I32),
                                           vAll[:, 0:NT].bitcast(I32), 1,
                                           None, op0=ALU.arith_shift_right))
        V(lambda e, NT=NT: e.tensor_scalar(rsA[:, 0:NT].bitcast(I32),
                                           rsA[:, 0:NT].bitcast(I32), -1,
                                           0x5f3759df, op0=ALU.mult, op1=ALU.add))
        for _ in range(3):
            V(lambda e, NT=NT: e.tensor_tensor(nwt[:, 0:NT], rsA[:, 0:NT], rsA[:, 0:NT], op=ALU.mult))
            V(lambda e, NT=NT: e.tensor_tensor(nwt[:, 0:NT], nwt[:, 0:NT], vAll[:, 0:NT], op=ALU.mult))
            V(lambda e, NT=NT: e.tensor_scalar(nwt[:, 0:NT], nwt[:, 0:NT], -0.5, 1.5,
                                               op0=ALU.mult, op1=ALU.add))
            V(lambda e, NT=NT: e.tensor_tensor(rsA[:, 0:NT], rsA[:, 0:NT], nwt[:, 0:NT], op=ALU.mult))
        for tt in range(NT2):
            V(lambda e, tt=tt, base=base: e.tensor_scalar(
                ln2_tm[:, tt * D:(tt + 1) * D],
                xt2[:, base + tt * D: base + (tt + 1) * D],
                mAll[:, tt:tt + 1], rsA[:, tt:tt + 1],
                op0=ALU.subtract, op1=ALU.mult),
              inc=SM["norm2"].up(), drain=False)
        norm2_tok[c] = SM["norm2"].tok()

    def plan_T2(c):
        if c >= NC2:
            return
        par = c % 2
        fm0 = par * KD * T2
        for tt in range(NT2):
            b = tt % 2
            for k in range(KD):
                T(lambda e, tt=tt, k=k, b=b: e.transpose(
                    pT16[b][:, k * 128:(k + 1) * 128],
                    ln2_tm[:, tt * D + k * 128: tt * D + (k + 1) * 128], id16[:]),
                  waits=[(SM["norm2"], norm2_base[c] + tt + 1), bwait(("T16", b))] if k == 0 else (),
                  inc=SM["tT2"].up() if k == KD - 1 else None)
            tk = SM["tT2"].tok()
            dst = ln2_fm[:, fm0:fm0 + KD * T2].rearrange(
                "p (k t) -> p k t", t=T2)[:, :, tt * 128:(tt + 1) * 128]
            srcv = pT16[b][:, 0:KD * 128].rearrange("p (k t) -> p k t", t=128)
            V(lambda e, dst=dst, srcv=srcv: e.tensor_copy(dst, srcv),
              waits=[tk, guard2.get(("ln2_fm", par)) if tt == 0 else None,
                     ([(SM["xtsA"], SM["xtsA"].n), (SM["xtst"], SM["xtst"].n)]
                      if (c == 0 and tt == 0) else None)],
              inc=SM["tT2e"].up(), drain=False)
            bset(("T16", b), SM["tT2e"].tok())
        tT2e_tok[c] = SM["tT2e"].tok()
        guard2[("ln2_tm", 0)] = SM["tT2"].tok()

    def plan_M5(c):
        if c >= NC2:
            return
        par = c % 2
        fm0 = par * KD * T2
        g0 = par * 2 * KD * T2
        gel_base[c] = SM["gel"].n
        for slot in range(2):
            for m in range(KD):
                b = m % 4
                for k in range(KD):
                    waits = [gw_tok[("W1", slot, k)] if m == 0 else None]
                    if k == 0:
                        waits += [bwait(("h", b)),
                                  tT2e_tok[c] if (slot == 0 and m == 0) else None]
                    T(lambda e, slot=slot, m=m, k=k, b=b, fm0=fm0: e.matmul(
                        PB[b][:, 0:T2],
                        W1g[:, (slot * KD + k) * D + m * 128:(slot * KD + k) * D + (m + 1) * 128],
                        ln2_fm[:, fm0 + k * T2: fm0 + (k + 1) * T2],
                        start=(k == 0), stop=(k == KD - 1)),
                      waits=waits,
                      inc=SM["m5"].up() if k == KD - 1 else None)
                tm = SM["m5"].tok()
                C(lambda e, slot=slot, m=m, b=b, g0=g0: e.activation(
                    gel[:, g0 + (slot * KD + m) * T2: g0 + (slot * KD + m + 1) * T2],
                    PB[b][:, 0:T2],
                    AF.Gelu, bias=b1g[:, slot * KD + m:slot * KD + m + 1], scale=1.0),
                  waits=([tm, t_bias, guard2.get(("gel", par))]
                         + ([m4_tok[NC1 - 1], (SM["xtst"], SM["xtst"].n)] if c < 2 else [])
                         ) if (slot == 0 and m == 0) else [tm],
                  inc=SM["gel"].up(), drain=False)
                bset(("h", b), SM["gel"].tok())
        m5_tok[c] = SM["m5"].tok()
        guard2[("ln2_fm", par)] = m5_tok[c]

    def plan_M6(c):
        par = c % 2
        g0 = par * 2 * KD * T2
        for m2 in range(KD):
            b = m2 % 2
            for i, (slot, kk) in enumerate([(s, k) for s in range(2) for k in range(KD)]):
                waits = [gw_tok[("W2", slot, kk)] if m2 == 0 else None,
                         (SM["gel"], gel_base[c] + slot * KD + kk + 1) if m2 == 0 else None]
                if i == 0:
                    waits.append(bwait(("o", b)))
                T(lambda e, m2=m2, slot=slot, kk=kk, b=b, i=i, g0=g0: e.matmul(
                    pUY[b][:, 0:T2],
                    W2g[:, (slot * KD + kk) * D + m2 * 128:(slot * KD + kk) * D + (m2 + 1) * 128],
                    gel[:, g0 + (slot * KD + kk) * T2: g0 + (slot * KD + kk + 1) * T2],
                    start=(i == 0), stop=(i == 2 * KD - 1)),
                  waits=waits,
                  inc=SM["m6"].up() if i == 2 * KD - 1 else None)
            tm = SM["m6"].tok()
            C(lambda e, m2=m2, b=b: e.activation(
                o_s[:, m2 * T2:(m2 + 1) * T2], pUY[b][:, 0:T2], AF.Identity,
                bias=b2sum[:, m2:m2 + 1], scale=0.5),
              waits=([tm, guard2.get(("o_s", 0))]
                     + ([m4_tok[NC1 - 1]] if c == 0 else [])) if m2 == 0 else [tm],
              inc=SM["oev"].up(), drain=False)
            bset(("o", b), SM["oev"].tok())
        m6_tok[c] = SM["m6"].tok()
        guard2[("gel", par)] = m6_tok[c]

    def plan_To2(c):
        par = c % 3
        t_oev = SM["oev"].tok()
        gi = 0
        for tt in range(NT2):
            b = tt % 2
            for k in range(KD):
                T(lambda e, tt=tt, k=k, b=b: e.transpose(
                    pT16[b][:, k * 128:(k + 1) * 128],
                    o_s[:, k * T2 + tt * 128: k * T2 + (tt + 1) * 128], id16[:]),
                  waits=[t_oev, bwait(("T16", b))] if k == 0 else (),
                  inc=SM["to2"].up() if k == KD - 1 else None)
            tk = SM["to2"].tok()
            V(lambda e, tt=tt, b=b, par=par: e.tensor_tensor(
                out_s[:, tt * D:(tt + 1) * D], pT16[b][:, 0:D],
                xt2[:, (par * NT2 + tt) * D:(par * NT2 + tt + 1) * D],
                op=ALU.add),
              waits=[tk, guard2.get(("out", 0)) if gi == 0 else None,
                     ([m4_tok[NC1 - 1], y2_tok[NC1 - 1]] if (c == 0 and gi == 0) else None)],
              inc=SM["oute"].up(), drain=False)
            bset(("T16", b), SM["oute"].tok())
            gi += 1
            tout_t = SM["oute"].tok()
            DMA("sync", lambda e, c=c, tt=tt: e.dma_start(
                out=d_y[c * T2 + tt * 128: c * T2 + (tt + 1) * 128, :],
                in_=out_s[:, tt * D:(tt + 1) * D]),
                waits=[tout_t], sem=SM["yst"])
        guard2[("out", 0)] = SM["yst"].tok()
        guard2[("o_s", 0)] = SM["to2"].tok()
        guard2[("xt2", par)] = SM["oute"].tok()

    load_xt2(0)
    plan_norm2(0)
    plan_T2(0)
    plan_M5(0)
    load_xt2(1)
    plan_norm2(1)
    plan_T2(1)
    for c in range(NC2):
        load_xt2(c + 2)
        plan_norm2(c + 2)
        plan_M5(c + 1)
        plan_T2(c + 2)
        plan_M6(c)
        plan_To2(c)

    P.add("sync", lambda e: e.nop(), [(SM["yst"], SM["yst"].n)], None, False)

    with nc.Block() as blk:
        @blk.sync
        def _(e):
            P.replay("sync", e)

        @blk.tensor
        def _(e):
            P.replay("tensor", e)

        @blk.vector
        def _(e):
            P.replay("vector", e)

        @blk.scalar
        def _(e):
            P.replay("scalar", e)

        @blk.gpsimd
        def _(e):
            P.replay("gpsimd", e)

    p2.close()
    es.close()
    return nc


def make_core_inputs(xb, d, S):
    conv_w = np.asarray(d["L0_attn_conv_w"], np.float32)
    conv_fm = conv_w[:, 0, :].reshape(KI, 128, 4).transpose(1, 0, 2).reshape(128, KI * 4)
    convb_fm = np.asarray(d["L0_attn_conv_b"], np.float32).reshape(KI, 128).T
    b1rl = np.asarray(d["L0_ffn_b1"], np.float32).reshape(NE, KD, 128).transpose(2, 1, 0).reshape(128, KD * NE)
    b2rl = np.asarray(d["L0_ffn_b2"], np.float32).reshape(NE, KD, 128).transpose(2, 1, 0).reshape(128, KD * NE)
    return {
        "x": np.ascontiguousarray(xb[:S]),
        "w_in": np.ascontiguousarray(np.asarray(d["L0_attn_w_in"], np.float32)),
        "w_out": np.ascontiguousarray(np.asarray(d["L0_attn_w_out"], np.float32)),
        "w_B": np.ascontiguousarray(np.asarray(d["L0_attn_w_B"], np.float32)),
        "w_C": np.ascontiguousarray(np.asarray(d["L0_attn_w_C"], np.float32)),
        "conv_fm": np.ascontiguousarray(conv_fm),
        "convb_fm": np.ascontiguousarray(convb_fm),
        "Wr": np.ascontiguousarray(np.asarray(d["L0_ffn_Wr"], np.float32)),
        "br": np.ascontiguousarray(np.asarray(d["L0_ffn_br"], np.float32).reshape(1, NE)),
        "W1": np.ascontiguousarray(np.asarray(d["L0_ffn_W1"], np.float32)),
        "W2": np.ascontiguousarray(np.asarray(d["L0_ffn_W2"], np.float32)),
        "b1rl": np.ascontiguousarray(b1rl),
        "b2rl": np.ascontiguousarray(b2rl),
        "id32": np.eye(DS, dtype=np.float32),
        "id16": np.eye(128, dtype=bf16),
    }


def kernel(**inputs):
    x = np.asarray(inputs["L0_attn_x"], np.float32)
    B, S, _ = x.shape
    nc = build_program(S=S)
    in_maps = [make_core_inputs(x[b], inputs, S) for b in range(B)]
    res = run_bass_kernel_spmd(nc, in_maps, core_ids=list(range(B)))
    return np.stack([np.asarray(res.results[b]["y"], np.float32) for b in range(B)], axis=0)
